# version 07
# baseline (speedup 1.0000x reference)
import numpy as np
import concourse.bass as bass
import concourse.mybir as mybir
from concourse.bass_utils import run_bass_kernel_spmd

F32 = mybir.dt.float32
BF16 = mybir.dt.bfloat16
I8 = mybir.dt.int8
ALU = mybir.AluOpType
AF = mybir.ActivationFunctionType
AXX = mybir.AxisListType.X

D = 2048
INTER = 8192
R = 1024
EPS = 1e-5
SCALE = 128.0 ** -0.5

ENGS = ("g", "y", "t", "v", "s")


class Plan:

    def __init__(self):
        self.steps = {e: [] for e in ENGS}
        self.cnt = {}

    def tok(self, sem):
        return (sem, self.cnt.get(sem, 0))

    def emit(self, eng, fn, waits=(), inc=None, amt=1):
        ws = [w for w in waits if w is not None and w[1] > 0]
        for s, v in ws:
            assert v <= self.cnt.get(s, 0), f"wait on future {s}={v} (now {self.cnt.get(s,0)})"
        tok = None
        if inc is not None:
            self.cnt[inc] = self.cnt.get(inc, 0) + amt
            tok = (inc, self.cnt[inc])
        self.steps[eng].append((fn, ws, inc, amt))
        return tok

    def dma(self, eng, sem, out, in_, waits=(), **kw):
        return self.emit(eng, lambda e: e.dma_start(out, in_, **kw), waits, sem, 16)


def build_program(dbg=False, stop=None):
    nc = bass.Bass("TRN2", target_bir_lowering=False, detect_race_conditions=False)

    xin = nc.dram_tensor("x", [R, D], F32, kind="ExternalInput")
    wq = nc.dram_tensor("wq", [D, D], F32, kind="ExternalInput")
    wk = nc.dram_tensor("wk", [D, D], F32, kind="ExternalInput")
    wv = nc.dram_tensor("wv", [D, D], F32, kind="ExternalInput")
    wo = nc.dram_tensor("wo", [D, D], F32, kind="ExternalInput")
    bq = nc.dram_tensor("bq", [16, 128], F32, kind="ExternalInput")
    bk = nc.dram_tensor("bk", [16, 128], F32, kind="ExternalInput")
    bv = nc.dram_tensor("bv", [16, 128], F32, kind="ExternalInput")
    bo = nc.dram_tensor("bo", [1, D], F32, kind="ExternalInput")
    wg = nc.dram_tensor("wg", [D, INTER], I8, kind="ExternalInput")
    wu = nc.dram_tensor("wu", [D, INTER], I8, kind="ExternalInput")
    wd = nc.dram_tensor("wd", [INTER, D], I8, kind="ExternalInput")
    sg = nc.dram_tensor("sg", [64, 128], F32, kind="ExternalInput")
    su = nc.dram_tensor("su", [64, 128], F32, kind="ExternalInput")
    sd = nc.dram_tensor("sd", [1, D], F32, kind="ExternalInput")
    msk_in = nc.dram_tensor("msk", [128, 1], F32, kind="ExternalInput")
    y = nc.dram_tensor("y", [R, D], F32, kind="ExternalOutput")

    w_bf = {
        "q": nc.dram_tensor("wq_bf", [D, D], BF16),
        "k": nc.dram_tensor("wk_bf", [D, D], BF16),
        "v": nc.dram_tensor("wv_bf", [D, D], BF16),
        "o": nc.dram_tensor("wo_bf", [D, D], BF16),
    }
    bo_bf_d = nc.dram_tensor("bo_bf_d", [1, D], BF16)
    if dbg:
        dbg_xnt = nc.dram_tensor("dbg_xnt", [128, 16 * R], BF16, kind="ExternalOutput")
        dbg_pkv = nc.dram_tensor("dbg_pkv", [128, 16 * R], BF16, kind="ExternalOutput")
        dbg_tot = nc.dram_tensor("dbg_tot", [128, 16], F32, kind="ExternalOutput")
        dbg_ini = nc.dram_tensor("dbg_ini", [128, 16], F32, kind="ExternalOutput")
        dbg_ctx = nc.dram_tensor("dbg_ctx", [128, 16 * R], BF16, kind="ExternalOutput")
        dbg_xn2 = nc.dram_tensor("dbg_xn2", [128, 16 * R], BF16, kind="ExternalOutput")
        dbg_x2 = nc.dram_tensor("dbg_x2", [R, D], F32, kind="ExternalOutput")
        dbg_att = nc.dram_tensor("dbg_att", [128, 4096], F32, kind="ExternalOutput")
    cc_in = nc.dram_tensor("cc_in", [128, 16], F32)
    cc_out = nc.dram_tensor("cc_out", [256, 16], F32)

    plan = Plan()
    E = plan.emit
    DMA = plan.dma
    TOK = plan.tok

    from contextlib import ExitStack
    with ExitStack() as stack:
        en = stack.enter_context
        resid = en(nc.sbuf_tensor("resid", [128, 8 * D], F32))
        xnt = en(nc.sbuf_tensor("xnt", [128, 16 * R], BF16))
        pkv = en(nc.sbuf_tensor("pkv", [128, 16 * R], BF16))
        wtb = en(nc.sbuf_tensor("wtb", [128, 16384], BF16))
        a5 = en(nc.sbuf_tensor("a5", [128, 4096], F32))
        scr = en(nc.sbuf_tensor("scr", [128, 2048], F32))
        identb = en(nc.sbuf_tensor("identb", [128, 128], BF16))
        identf = en(nc.sbuf_tensor("identf", [128, 128], F32))
        onesb = en(nc.sbuf_tensor("onesb", [1, 128], BF16))
        bcol = en(nc.sbuf_tensor("bcol", [128, 48], F32))
        bo_hi = en(nc.sbuf_tensor("bo_hi", [1, D], BF16))
        sgcol = en(nc.sbuf_tensor("sgcol", [128, 64], F32))
        sucol = en(nc.sbuf_tensor("sucol", [128, 64], F32))
        msk = en(nc.sbuf_tensor("msk_sb", [128, 1], F32))
        totals = en(nc.sbuf_tensor("totals", [128, 16], F32))
        initc = en(nc.sbuf_tensor("initc", [128, 16], F32))
        stats = en(nc.sbuf_tensor("stats", [128, 64], F32))
        stage1 = en(nc.sbuf_tensor("stage1", [64, 128], F32))
        dly_v = en(nc.sbuf_tensor("dly_v", [128, 560], F32))
        wnat = en(nc.sbuf_tensor("wnat", [128, 4096], BF16))
        dly_s = en(nc.sbuf_tensor("dly_s", [128, 512], F32))
        psA = en(nc.psum_tensor("psA", [128, 1024], F32))
        psB = en(nc.psum_tensor("psB", [128, 1024], F32))
        psS = en(nc.psum_tensor("psS", [128, 1024], F32))
        psT = en(nc.psum_tensor("psT", [128, 2048], BF16))
        psX = psS
        sem_names = ["t", "v", "s", "g", "cc",
                     "dx", "dck", "dcv", "dcq", "dco", "dcb", "dmisc", "dstg",
                     "dw0", "dw1", "dcc", "dst", "dg0", "dg1", "dg2",
                     "dd0", "dd1", "dsd", "dr0", "dr1", "df0", "df1"]
        sems = {nm: en(nc.semaphore("sem_" + nm)) for nm in sem_names}
        block = en(nc.Block())

        a5b = a5.bitcast(BF16)
        fuse = resid.bitcast(BF16)
        otile = pkv.bitcast(F32)
        scrb = scr.bitcast(BF16)
        wtbf = wtb.bitcast(F32)
        assert list(a5b.shape) == [128, 8192], a5b.shape

        tg1 = E("g", lambda e: e.memset(identb[:], 0.0), inc="g")
        tg2 = E("g", lambda e: e.memset(identf[:], 0.0), inc="g")
        tg3 = E("g", lambda e: e.affine_select(out=identb[:], in_=identb[:],
                                               compare_op=ALU.not_equal, fill=1.0, base=0,
                                               pattern=[[-1, 128]], channel_multiplier=1),
                waits=[tg1], inc="g")
        tg4 = E("g", lambda e: e.affine_select(out=identf[:], in_=identf[:],
                                               compare_op=ALU.not_equal, fill=1.0, base=0,
                                               pattern=[[-1, 128]], channel_multiplier=1),
                waits=[tg2], inc="g")
        t_ident = E("g", lambda e: e.memset(onesb[:], 1.0), waits=[tg3, tg4], inc="g")
        E("v", lambda e: e.memset(dly_v[:], 0.0))
        E("s", lambda e: e.memzero(dly_s[:]))
        E("v", lambda e: e.memset(stats[:, 58:59], 1.0 / D))
        E("v", lambda e: e.memset(stats[:, 59:60], -1.0))
        E("v", lambda e: e.memset(stats[:, 60:61], EPS))
        E("v", lambda e: e.memset(stats[:, 61:62], -0.5))
        t_consts = E("v", lambda e: e.tensor_add(dly_v[:], dly_v[:], dly_v[:]), inc="v")

        for rt in range(8):
            DMA("g", "dx", resid[:, rt * D:(rt + 1) * D],
                xin[rt * 128:(rt + 1) * 128, :])
        t_x = TOK("dx")
        wsrc_of = {"k": wk, "v": wv, "q": wq, "o": wo}
        t_bo_cast = DMA("g", "dcb", bo_bf_d[:, :], bo[:, :])

        DMA("y", "dmisc", msk[:], msk_in[:, :])
        t_msk = TOK("dmisc")

        prep_done = []
        stage_free = None
        for src, n, dst in ((bq, 16, bcol[:, 0:16]), (bk, 16, bcol[:, 16:32]),
                            (bv, 16, bcol[:, 32:48]), (sg, 64, sgcol[:, 0:64]),
                            (su, 64, sucol[:, 0:64])):
            tl = DMA("y", "dstg", stage1[0:n, :], src[0:n, :], waits=[stage_free])
            tt = E("t", lambda e, n_=n: e.transpose(psX[0:128, 0:n_], stage1[0:n_, :],
                                                    identf[0:n_, 0:n_]),
                   waits=[tl, t_ident, TOK("s")], inc="t")
            E("s", lambda e, n_=n, d_=dst: e.copy(d_, psX[:, 0:n_]), waits=[tt])
            ts_ = E("s", lambda e: e.copy(dly_s[:], dly_s[:]), inc="s")
            prep_done.append(ts_)
            stage_free = ts_
        prep_q, prep_k, prep_v, prep_sg, prep_su = prep_done

        def stats_chain(c, width, wv_tok):
            cw = slice(0, 0)
            def a_(out_c, in_c, func, bias=None, scale=None, w=width):
                def fn(e, oc=out_c, ic=in_c, f=func, b=bias, sc=scale, w_=w):
                    kw = {}
                    if b is not None:
                        kw["bias"] = stats[:, b:b + 1] if isinstance(b, int) else b
                    if sc is not None:
                        kw["scale"] = stats[:, sc:sc + 1] if isinstance(sc, int) else sc
                    return e.activation(stats[:, oc:oc + w_], stats[:, ic:ic + w_], f, **kw)
                return fn
            E("s", a_(24 + c, c, AF.Identity, scale=stats[:, 58:59]),
              waits=[wv_tok, t_consts])
            E("s", a_(16 + c, 8 + c, AF.Identity, scale=stats[:, 58:59]))
            E("s", a_(8 + c, 24 + c, AF.Square))
            E("s", a_(8 + c, 8 + c, AF.Identity, scale=stats[:, 59:60]))
            for j in range(width):
                E("s", a_(32 + c + j, 16 + c + j, AF.Identity, bias=8 + c + j, w=1))
            E("s", a_(32 + c, 32 + c, AF.Ln, bias=stats[:, 60:61]))
            E("s", a_(32 + c, 32 + c, AF.Exp, scale=stats[:, 61:62]))
            E("s", a_(48 + c, 24 + c, AF.Identity, scale=stats[:, 59:60]))
            for j in range(width):
                E("s", a_(40 + c + j, 48 + c + j, AF.Identity, scale=32 + c + j, w=1))
            return E("s", lambda e: e.copy(dly_s[:], dly_s[:]), inc="s")

        scr_free = [("v", 0)]

        def ln_sums(c, src_ap, first_waits):
            E("v", lambda e: e.reduce_sum(stats[:, c:c + 1], src_ap, axis=AXX),
              waits=list(first_waits))
            E("s", lambda e: e.activation(scr[:, 0:1024], src_ap[:, 0:1024], AF.Square),
              waits=list(first_waits) + [scr_free[0]])
            sqb = E("s", lambda e: e.activation(scr[:, 1024:2048], src_ap[:, 1024:2048],
                                                AF.Square), inc="s")
            E("v", lambda e: e.reduce_sum(stats[:, 8 + c:9 + c], scr[:], axis=AXX),
              waits=[sqb])
            tok = E("v", lambda e: e.tensor_add(dly_v[:], dly_v[:], dly_v[:]), inc="v")
            scr_free[0] = tok
            return tok

        def ln_stats(c, src_ap, first_waits):
            return stats_chain(c, 1, ln_sums(c, src_ap, first_waits))

        def transpose_16(src_bf, rt, first_wait, psT_free):
            last_ev = None
            for b4 in range(4):
                sl = b4 % 2
                tt = None
                for j in range(4):
                    m = b4 * 4 + j
                    tt = E("t", lambda e, i_=src_bf[:, m * 128:(m + 1) * 128],
                           o_=psT[:, sl * 1024 + j * 128:sl * 1024 + (j + 1) * 128]:
                           e.transpose(o_, i_, identb[:]),
                           waits=[first_wait, psT_free[sl], t_ident],
                           inc="t" if j == 3 else None)
                ev = E("v", lambda e, m0=b4 * 4, r0=rt * 128, sl_=sl: e.tensor_copy(
                    bass.AP(xnt, m0 * R + r0, [[16 * R, 128], [R, 4], [1, 128]]),
                    bass.AP(psT, sl_ * 1024, [[2048, 128], [128, 4], [1, 128]])),
                    waits=[tt], inc="v")
                psT_free[sl] = ev
                last_ev = ev
            return last_ev

        for rt in range(8):
            ln_sums(rt, resid[:, rt * D:(rt + 1) * D], [t_x])
        t_stats1 = stats_chain(0, 8, TOK("v"))

        psT_free = [TOK("v"), TOK("v")]
        xn_free = [TOK("t"), TOK("t")]
        for rt in range(8):
            par = rt % 2
            xnb = a5b[:, par * 2048:(par + 1) * 2048]
            tn = E("s", lambda e, rt_=rt, o_=xnb: e.activation(
                o_, resid[:, rt_ * D:(rt_ + 1) * D], AF.Identity,
                bias=stats[:, 40 + rt_:41 + rt_], scale=stats[:, 32 + rt_:33 + rt_]),
                waits=[t_stats1, xn_free[par]], inc="s")
            ev = transpose_16(xnb, rt, tn, psT_free)
            xn_free[par] = TOK("t")
        bar_p1 = [TOK("t"), TOK("v"), TOK("s")]
        if dbg:
            bar_p1.append(DMA("y", "dmisc", dbg_xnt[:, :], xnt[:], waits=bar_p1))

        _lvl = {"p1": 1, "p2": 2, "p3": 3, None: 9}[stop]
        if dbg and _lvl == 1:
            for rt_ in range(8):
                DMA("y", "dmisc", dbg_x2[rt_ * 128:(rt_ + 1) * 128, :],
                    resid[:, rt_ * D:(rt_ + 1) * D], waits=bar_p1)
            DMA("y", "dmisc", dbg_tot[:, :], stats[:, 0:16], waits=bar_p1)
            DMA("y", "dmisc", dbg_ini[:, :], stats[:, 24:40], waits=bar_p1)
            DMA("y", "dmisc", dbg_ctx[:, 0:32], stats[:, 48:64].bitcast(BF16), waits=bar_p1)
            DMA("y", "dmisc", dbg_pkv[:, 0:4096], a5b[:, 0:4096], waits=bar_p1)
        if _lvl >= 2:
            wslot_free = [("t", 0), ("t", 0)]
            wnat_free = [("t", 0), ("t", 0)]

            def load_wt_group_pe(key, g, p):
                wsrc = wsrc_of[key]
                ev = None
                for ib in range(4):
                    eb = g * 4 + ib
                    ns = eb % 2
                    ld = DMA("g", "dg%d" % ns,
                             wnat[:, ns * 2048:(ns + 1) * 2048],
                             wsrc[eb * 128:(eb + 1) * 128, :],
                             waits=[wnat_free[ns]])
                    for tb in range(4):
                        psl = tb % 2
                        tt = None
                        for j in range(4):
                            t_ = tb * 4 + j
                            tt = E("t", lambda e, ns_=ns, t2=t_, psl_=psl, j_=j:
                                   e.transpose(
                                       psT[:, psl_ * 1024 + j_ * 128:
                                           psl_ * 1024 + (j_ + 1) * 128],
                                       wnat[:, ns_ * 2048 + t2 * 128:
                                            ns_ * 2048 + (t2 + 1) * 128],
                                       identb[:]),
                                   waits=[ld, psT_free[psl], t_ident],
                                   inc="t" if j == 3 else None)
                        ev = E("v", lambda e, p_=p, tb_=tb, ib_=ib, psl_=psl:
                               e.tensor_copy(
                                   bass.AP(wtb, p_ * 8192 + tb_ * 4 * 512 + ib_ * 128,
                                           [[16384, 128], [512, 4], [1, 128]]),
                                   bass.AP(psT, psl_ * 1024,
                                           [[2048, 128], [128, 4], [1, 128]])),
                               waits=[tt, wslot_free[p] if (ib == 0 and tb == 0) else None],
                               inc="v")
                        psT_free[psl] = ev
                    wnat_free[ns] = tt
                return ev

            def load_wt_group(wbf_dram, g, extra_wait):
                p = g % 2
                first = True
                tok = None
                for d in range(16):
                    tok = DMA("y", "dw%d" % p,
                              wtb[:, p * 8192 + d * 512:p * 8192 + d * 512 + 512],
                              wbf_dram[g * 512:(g + 1) * 512, d * 128:(d + 1) * 128],
                              waits=[wslot_free[p], extra_wait] if first else (),
                              transpose=True)
                    first = False
                return tok

            def proj_mms(m, ps, extra_waits):
                p = (m // 4) % 2
                last = None
                for rh in range(2):
                    for d in range(16):
                        last = E("t", lambda e, d_=d, rh_=rh, m_=m, p_=p, ps_=ps: e.matmul(
                            ps_[:, rh_ * 512:(rh_ + 1) * 512],
                            wtb[:, p_ * 8192 + d_ * 512 + (m_ % 4) * 128:
                                p_ * 8192 + d_ * 512 + (m_ % 4) * 128 + 128],
                            xnt[:, d_ * R + rh_ * 512:d_ * R + rh_ * 512 + 512],
                            start=(d_ == 0), stop=(d_ == 15)),
                            waits=list(extra_waits) if (rh == 0 and d == 0) else (),
                            inc="t" if (rh == 1 and d == 15) else None)
                return last

            ps_free = {0: ("v", 0), 1: ("v", 0)}
            TEXP = [a5b[:, 0:1024], a5b[:, 1024:2048]]
            TRELU = a5b[:, 2048:3072]
            TV = [a5b[:, 3072:4096], a5b[:, 4096:5120]]
            PHIQ = a5b[:, 5120:6144]
            texp_free = [TOK("v"), TOK("v")]
            tv_free = [TOK("v"), TOK("v")]

            def wt_pass(key, evac, first_extra, preloaded=None):
                loaded = preloaded if preloaded is not None else {
                    0: load_wt_group_pe(key, 0, 0),
                    1: load_wt_group_pe(key, 1, 1)}
                for m in range(16):
                    g = m // 4
                    ps = psA if m % 2 == 0 else psB
                    extra = [loaded[g], ps_free[m % 2]] + (first_extra if m == 0 else [])
                    tmm = proj_mms(m, ps, extra)
                    if m % 4 == 3:
                        wslot_free[g % 2] = tmm
                        if g + 2 < 4:
                            loaded[g + 2] = load_wt_group_pe(key, g + 2, g % 2)
                    evac(m, ps, tmm)

            def evac_k(m, ps, tmm):
                par = m % 2
                te = E("s", lambda e, ps_=ps, m_=m, p_=par: e.activation(
                    TEXP[p_], ps_[:], AF.Exp, bias=bcol[:, 16 + m_:17 + m_]),
                    waits=[tmm, texp_free[par], prep_k], inc="s")
                tr = E("v", lambda e, ps_=ps, m_=m: e.tensor_scalar(
                    TRELU, ps_[:], bcol[:, 16 + m_:17 + m_], 0.0, ALU.add, ALU.max),
                    waits=[te], inc="v")
                ps_free[m % 2] = tr
                ph = E("v", lambda e, m_=m, p_=par: e.scalar_tensor_tensor(
                    pkv[:, m_ * R:(m_ + 1) * R], TEXP[p_], 1.0, TRELU, ALU.min, ALU.add),
                    inc="v")
                texp_free[par] = ph

            wt_pass("k", evac_k, bar_p1)

            tot_toks = []

            def evac_v(m, ps, tmm):
                par = m % 2
                tv_ = E("s", lambda e, ps_=ps, m_=m, p_=par: e.activation(
                    TV[p_], ps_[:], AF.Identity, bias=bcol[:, 32 + m_:33 + m_]),
                    waits=[tmm, tv_free[par], prep_v], inc="s")
                ps_free[m % 2] = tv_
                mu = E("v", lambda e, m_=m, p_=par: e.tensor_mul(
                    pkv[:, m_ * R:(m_ + 1) * R], pkv[:, m_ * R:(m_ + 1) * R], TV[p_]),
                    waits=[tv_], inc="v")
                tv_free[par] = mu
                E("v", lambda e, m_=m: e.reduce_sum(
                    totals[:, m_:m_ + 1], pkv[:, m_ * R:(m_ + 1) * R], axis=AXX))
                if m == 15:
                    tot_toks.append(E("v", lambda e: e.tensor_add(dly_v[:], dly_v[:], dly_v[:]),
                                     inc="v"))
                else:
                    tot_toks.append(TOK("v"))

            wt_pass("v", evac_v, [])

            q_preload = {0: load_wt_group_pe("q", 0, 0),
                         1: load_wt_group_pe("q", 1, 1)}

            if dbg:
                dtok = DMA("y", "dmisc", dbg_pkv[:, :], pkv[:], waits=[tot_toks[-1]])
                DMA("y", "dmisc", dbg_tot[:, :], totals[:], waits=[tot_toks[-1]])

            d1 = DMA("y", "dcc", cc_in[:, :], totals[:, 0:16], waits=[tot_toks[-1]])
            ccd = E("g", lambda e: e.collective_compute(
                "AllGather", ALU.bypass, replica_groups=[[0, 1], [2, 3], [4, 5], [6, 7]],
                ins=[cc_in.ap().opt()], outs=[cc_out.ap().opt()]),
                waits=[d1], inc="cc")
            d2 = DMA("y", "dcc", initc[:, 0:16], cc_out[0:128, :], waits=[ccd])
            E("s", lambda e: e.activation(initc[:, 0:16], initc[:, 0:16], AF.Identity,
                                          scale=msk[:]),
              waits=[d2, t_msk])
            t_init = E("s", lambda e: e.copy(dly_s[:], dly_s[:]), inc="s")
            t_bo = DMA("y", "dmisc", bo_hi[:], bo_bf_d[:, :], waits=[t_bo_cast])
            if dbg:
                DMA("y", "dmisc", dbg_ini[:, :], initc[:], waits=[t_init])

            ctx_last = [None]

            def evac_q(m, ps, tmm):
                par = m % 2
                te = E("s", lambda e, ps_=ps, m_=m, p_=par: e.activation(
                    TEXP[p_], ps_[:], AF.Exp, bias=bcol[:, m_:1 + m_]),
                    waits=[tmm, texp_free[par], prep_q], inc="s")
                tr = E("v", lambda e, ps_=ps, m_=m: e.tensor_scalar(
                    TRELU, ps_[:], bcol[:, m_:1 + m_], 0.0, ALU.add, ALU.max),
                    waits=[te], inc="v")
                ps_free[m % 2] = tr
                E("v", lambda e, p_=par: e.scalar_tensor_tensor(
                    PHIQ, TEXP[p_], 1.0, TRELU, ALU.min, ALU.add))
                E("v", lambda e, m_=m: e.tensor_tensor_scan(
                    psS[:], pkv[:, m_ * R:(m_ + 1) * R], pkv[:, m_ * R:(m_ + 1) * R],
                    initc[:, m_:m_ + 1], ALU.add, ALU.bypass),
                    waits=[t_init] + ([dtok] if dbg else []))
                cx = E("v", lambda e, m_=m: e.scalar_tensor_tensor(
                    pkv[:, m_ * R:(m_ + 1) * R], psS[:], SCALE, PHIQ, ALU.mult, ALU.mult),
                    inc="v")
                texp_free[par] = cx
                ctx_last[0] = cx

            wt_pass("q", evac_q, [], preloaded=q_preload)
            bar_ctx = ctx_last[0]


        if _lvl >= 3:
            PS3 = [psA[:, 0:512], psA[:, 512:1024], psB[:, 0:512], psB[:, 512:1024]]
            ps3_free = {i: TOK("v") for i in range(4)}
            x2_store = []
            tnorm_toks = []
            a5_free = {0: TOK("v"), 1: TOK("v")}

            for qq in range(4):
                wo_loaded = {0: load_wt_group_pe("o", 0, 0),
                             1: load_wt_group_pe("o", 1, 1)}
                evac_toks = {0: [], 1: []}
                for g in range(4):
                    p = g % 2
                    for rtw in range(2):
                        rt = qq * 2 + rtw
                        slot = (g * 2 + rtw) % 4
                        for d in range(16):
                            E("t", lambda e, d_=d, rt_=rt, p_=p, sl=slot: e.matmul(
                                PS3[sl],
                                pkv[:, d_ * R + rt_ * 128:d_ * R + rt_ * 128 + 128],
                                wtb[:, p_ * 8192 + d_ * 512:p_ * 8192 + d_ * 512 + 512],
                                start=(d_ == 0), stop=False),
                                waits=[wo_loaded[g], ps3_free[slot], bar_ctx, t_bo]
                                if d == 0 else ())
                        last = E("t", lambda e, g_=g, sl=slot: e.matmul(
                            PS3[sl], onesb[0:1, :], bo_hi[0:1, g_ * 512:(g_ + 1) * 512],
                            start=False, stop=True), inc="t")
                        ev = E("s", lambda e, g_=g, rtw_=rtw, sl=slot: e.copy(
                            a5[:, rtw_ * 2048 + g_ * 512:rtw_ * 2048 + g_ * 512 + 512],
                            PS3[sl]), waits=[last, a5_free[rtw]], inc="s")
                        ps3_free[slot] = ev
                        evac_toks[rtw].append(ev)
                    wslot_free[p] = last
                    if g + 2 < 4:
                        wo_loaded[g + 2] = load_wt_group_pe("o", g + 2, g % 2)
                if dbg and qq == 0:
                    datt = DMA("y", "dmisc", dbg_att[:, :], a5[:],
                               waits=evac_toks[0] + evac_toks[1])
                for rtw in range(2):
                    rt = qq * 2 + rtw
                    stag = a5[:, rtw * 2048:(rtw + 1) * 2048]
                    tn2 = ln_stats(rtw, stag, evac_toks[rtw] + [a5_free[rtw]] +
                                   ([datt] if (dbg and qq == 0) else []))
                    E("v", lambda e, s_=stag, c=rtw: e.tensor_scalar(
                        s_, s_, stats[:, 48 + c:49 + c], stats[:, 32 + c:33 + c],
                        ALU.add, ALU.mult), waits=[tn2])
                    add_t = E("v", lambda e, s_=stag, rt_=rt: e.tensor_add(
                        resid[:, rt_ * D:(rt_ + 1) * D], resid[:, rt_ * D:(rt_ + 1) * D], s_),
                        inc="v")
                    a5_free[rtw] = add_t
                    x2_store.append(DMA("y", "dst", y[rt * 128:(rt + 1) * 128, :],
                                        resid[:, rt * D:(rt + 1) * D], waits=[add_t]))
                    tn3 = ln_stats(rtw + 2, resid[:, rt * D:(rt + 1) * D], [add_t])
                    xn2b = fuse[:, rt * 4096:rt * 4096 + 2048]
                    tnorm = E("s", lambda e, rt_=rt, o_=xn2b, c=rtw + 2: e.activation(
                        o_, resid[:, rt_ * D:(rt_ + 1) * D], AF.Identity,
                        bias=stats[:, 40 + c:41 + c], scale=stats[:, 32 + c:33 + c]),
                        waits=[tn3, x2_store[-1]], inc="s")
                    tnorm_toks.append(tnorm)

            for rt in range(8):
                transpose_16(fuse[:, rt * 4096:rt * 4096 + 2048], rt,
                             tnorm_toks[rt], psT_free)

            bar_p3 = [TOK("t"), TOK("v"), TOK("s"), x2_store[-1]]
            if dbg:
                bar_p3.append(DMA("y", "dmisc", dbg_ctx[:, :], pkv[:], waits=[bar_ctx]))
                bar_p3.append(DMA("y", "dmisc", dbg_xn2[:, :], xnt[:], waits=bar_p3[:4]))
                for rt_ in range(8):
                    tok_ = DMA("y", "dmisc", dbg_x2[rt_ * 128:(rt_ + 1) * 128, :],
                               resid[:, rt_ * D:(rt_ + 1) * D], waits=bar_p3[:4])
                bar_p3.append(tok_)


        if _lvl >= 9:
            GU = [(wtb, 0), (wtb, 8192), (a5b, 0)]
            TG = [scrb[:, 0:512], scrb[:, 512:1024]]
            TU = [scrb[:, 1024:1536], scrb[:, 1536:2048]]

            def load_gu(wdram, s_, slot, wait_tok):
                parent, base = GU[slot]
                ncols = parent.shape[1]
                first = True
                tok = None
                for t2 in range(2):
                    tok = DMA("g", "dg%d" % slot,
                              bass.AP(parent, base + t2 * 8 * 512,
                                      [[ncols, 128], [512, 8], [1, 512]]),
                              bass.AP(wdram, t2 * 8 * 128 * INTER + s_ * 512,
                                      [[INTER, 128], [128 * INTER, 8], [1, 512]]),
                              waits=[wait_tok] if first else ())
                    first = False
                return tok

            def gu_slice(slot, d, icw):
                parent, base = GU[slot]
                return parent[:, base + d * 512 + icw * 128: base + d * 512 + icw * 128 + 128]

            bar_h = bar_p3
            h_slotC = bar_p3[0]
    
            x2l_free = [None, None]
            wd_last_prev = {0: bar_p3[0], 1: bar_p3[0]}
            psd_prev = None
            for h in range(2):
                slot_last = {0: wd_last_prev[0], 1: wd_last_prev[1], 2: h_slotC}
                loaded = {(0, 0): load_gu(wg, 0, 0, slot_last[0]),
                          (0, 1): load_gu(wu, 0, 1, slot_last[1])}
                slot_of = {(0, 0): 0, (0, 1): 1}
                for s_ in range(1, 16):
                    slot_of[(s_, 0)] = (2 * s_) % 3
                    slot_of[(s_, 1)] = (2 * s_ + 1) % 3
                tg_free = [TOK("v"), TOK("v")]
                tu_free = [TOK("v"), TOK("v")]
                if psd_prev is None:
                    psg_free = [TOK("s"), TOK("s")]
                    psu_free = [TOK("v"), TOK("v")]
                else:
                    psg_free = [psd_prev[0], psd_prev[2]]
                    psu_free = [psd_prev[1], psd_prev[3]]
                fuse_last = [None]
                for s_ in range(16):
                    slg, slu = slot_of[(s_, 0)], slot_of[(s_, 1)]
                    lastg = lastu = None
                    for icw in range(4):
                        ic = s_ * 4 + icw
                        par = ic % 2
                        psg = (psA if par == 0 else psB)[:, 0:512]
                        psu = (psA if par == 0 else psB)[:, 512:1024]
                        for d in range(16):
                            lastg = E("t", lambda e, d_=d, sl=slg, ic_=icw, ps_=psg, h_=h: e.matmul(
                                ps_, gu_slice(sl, d_, ic_),
                                xnt[:, d_ * R + h_ * 512:d_ * R + h_ * 512 + 512],
                                start=(d_ == 0), stop=(d_ == 15)),
                                waits=([loaded[(s_, 0)], psg_free[par]] +
                                       (bar_h if (h == 0 and s_ == 0 and icw == 0) else []))
                                if d == 0 else (),
                                inc="t" if d == 15 else None)
                        for d in range(16):
                            lastu = E("t", lambda e, d_=d, sl=slu, ic_=icw, ps_=psu, h_=h: e.matmul(
                                ps_, gu_slice(sl, d_, ic_),
                                xnt[:, d_ * R + h_ * 512:d_ * R + h_ * 512 + 512],
                                start=(d_ == 0), stop=(d_ == 15)),
                                waits=[loaded[(s_, 1)], psu_free[par]] if d == 0 else (),
                                inc="t" if d == 15 else None)
                        tg_t = E("s", lambda e, ps_=psg, ic_=ic, p_=par: e.activation(
                            TG[p_], ps_, AF.Silu, scale=sgcol[:, ic_:ic_ + 1]),
                            waits=[lastg, tg_free[par], prep_sg], inc="s")
                        psg_free[par] = tg_t
                        tu_t = E("v", lambda e, ps_=psu, ic_=ic, p_=par: e.tensor_scalar(
                            TU[p_], ps_, sucol[:, ic_:ic_ + 1], None, ALU.mult),
                            waits=[lastu, tu_free[par], prep_su], inc="v")
                        psu_free[par] = tu_t
                        f_t = E("v", lambda e, ic_=ic, p_=par: e.tensor_mul(
                            fuse[:, ic_ * 512:(ic_ + 1) * 512], TG[p_], TU[p_]),
                            waits=[tg_t], inc="v")
                        tg_free[par] = f_t
                        tu_free[par] = f_t
                        fuse_last[0] = f_t
                    slot_last[slg] = lastg
                    slot_last[slu] = lastu
                    if s_ + 1 < 16:
                        ng, nu = slot_of[(s_ + 1, 0)], slot_of[(s_ + 1, 1)]
                        loaded[(s_ + 1, 0)] = load_gu(wg, s_ + 1, ng, slot_last[ng])
                        loaded[(s_ + 1, 1)] = load_gu(wu, s_ + 1, nu, slot_last[nu])
                bar_4a = [TOK("t"), TOK("v"), TOK("s")]

                sdb = wtbf[:, 2048:4096]
                t_sdb = DMA("g", "dsd", sdb, bass.AP(sd, 0, [[0, 128], [1, 2048]]),
                            waits=[bar_4a[0]])
                WDSL = [(0, None), (2048, None)]

                def load_wd(s_, dgi, wait_tok):
                    p = s_ % 2
                    return DMA("g", "dd%d" % p,
                               bass.AP(wtb, WDSL[p][0], [[16384, 128], [512, 4], [1, 512]]),
                               bass.AP(wd, s_ * 512 * D + dgi * 512,
                                       [[D, 128], [128 * D, 4], [1, 512]]),
                               waits=[wait_tok])

                PSD = [psA[:, 0:512], psA[:, 512:1024], psB[:, 0:512], psB[:, 512:1024]]
                psd_free = {i: bar_4a[1] for i in range(4)}
                wd_last = {0: bar_4a[0], 1: bar_4a[0]}
                ffn_evac = {rs: [] for rs in range(4)}
                for dgi in range(4):
                    wloaded = {0: load_wd(0, dgi, wd_last[0]),
                               1: load_wd(1, dgi, wd_last[1])}
                    group_end = {}
                    for s_ in range(16):
                        p = s_ % 2
                        for icw in range(4):
                            ic = s_ * 4 + icw
                            for rs in range(4):
                                first = (s_ == 0 and icw == 0)
                                lastmm = (s_ == 15 and icw == 3)
                                if first:
                                    ws = [wloaded[p], psd_free[rs]] + \
                                         (bar_4a + [fuse_last[0]] if (rs == 0 and dgi == 0) else [])
                                elif icw == 0 and rs == 0:
                                    ws = [wloaded[p]]
                                else:
                                    ws = ()
                                tok = E("t", lambda e, ic_=ic, rs_=rs, p_=p, ic2=icw,
                                        fl=first, ll=lastmm: e.matmul(
                                    PSD[rs_],
                                    fuse[:, ic_ * 512 + rs_ * 128:ic_ * 512 + rs_ * 128 + 128],
                                    wtb[:, WDSL[p_][0] + ic2 * 512:WDSL[p_][0] + ic2 * 512 + 512],
                                    start=fl, stop=ll),
                                    waits=ws,
                                    inc="t" if (icw == 3 and rs == 3) else None)
                                if icw == 3 and rs == 3:
                                    group_end[s_] = tok
                        wd_last[p] = group_end[s_]
                        if s_ + 2 < 16:
                            wloaded[p] = load_wd(s_ + 2, dgi, wd_last[p])
                    for rs in range(4):
                        ev = E("v", lambda e, rs_=rs, dg_=dgi: e.scalar_tensor_tensor(
                            otile[:, rs_ * 2048 + dg_ * 512:rs_ * 2048 + dg_ * 512 + 512],
                            PSD[rs_], 1.0, sdb[:, dg_ * 512:(dg_ + 1) * 512],
                            ALU.bypass, ALU.mult),
                            waits=[group_end[15], t_sdb], inc="v")
                        ffn_evac[rs].append(ev)
                        psd_free[rs] = ev

                for rs in range(4):
                    rt = h * 4 + rs
                    par = rs % 2
                    ffn_t = otile[:, rs * 2048:(rs + 1) * 2048]
                    x2l = a5[:, par * 2048:(par + 1) * 2048]
                    rl = DMA("y", "dr%d" % par, x2l, y[rt * 128:(rt + 1) * 128, :],
                             waits=[TOK("dst"), x2l_free[par], bar_4a[0]])
                    tn4 = ln_stats(rs % 2, ffn_t, ffn_evac[rs])
                    E("v", lambda e, f_=ffn_t, c=rs % 2: e.tensor_scalar(
                        f_, f_, stats[:, 48 + c:49 + c], stats[:, 32 + c:33 + c],
                        ALU.add, ALU.mult), waits=[tn4])
                    fin = E("v", lambda e, f_=ffn_t, x_=x2l: e.tensor_add(x_, x_, f_),
                            waits=[rl], inc="v")
                    st = DMA("y", "df%d" % par, y[rt * 128:(rt + 1) * 128, :], x2l,
                             waits=[fin])
                    x2l_free[par] = st
                    if rs >= 2:
                        h_slotC = st
                wd_last_prev = {0: psd_free[3], 1: psd_free[3]}
                psd_prev = [psd_free[i] for i in range(4)]


        g_sems = ["dx", "dcb", "dg0", "dg1", "dg2", "dd0", "dd1", "dsd"]
        y_sems = ["dmisc", "dstg", "dw0", "dw1", "dcc", "dst", "dr0", "dr1",
                  "df0", "df1"]
        E("g", lambda e: e.nop(), waits=[TOK(sm) for sm in g_sems])
        E("y", lambda e: e.nop(), waits=[TOK(sm) for sm in y_sems])

        def runner(name, eng):
            seen = {}
            for fn, ws, inc, amt in plan.steps[name]:
                for s2, v2 in ws:
                    if seen.get(s2, 0) < v2:
                        eng.wait_ge(sems[s2], v2)
                        seen[s2] = v2
                ins = fn(eng)
                if inc is not None:
                    ins.then_inc(sems[inc], amt)

        @block.gpsimd
        def _(g_):
            runner("g", g_)

        @block.sync
        def _(y_):
            runner("y", y_)

        @block.tensor
        def _(t_):
            runner("t", t_)

        @block.vector
        def _(v_):
            runner("v", v_)

        @block.scalar
        def _(s_):
            runner("s", s_)

    return nc


_NC = None


def kernel(**inputs):
    global _NC
    if _NC is None:
        _NC = build_program()
    nc = _NC

    x = np.asarray(inputs["L0_attn_x"])
    base = {
        "wq": np.asarray(inputs["L0_attn_Wq"]),
        "wk": np.asarray(inputs["L0_attn_Wk"]),
        "wv": np.asarray(inputs["L0_attn_Wv"]),
        "wo": np.asarray(inputs["L0_attn_Wo"]),
        "bq": np.ascontiguousarray(np.asarray(inputs["L0_attn_bq"]).reshape(16, 128)),
        "bk": np.ascontiguousarray(np.asarray(inputs["L0_attn_bk"]).reshape(16, 128)),
        "bv": np.ascontiguousarray(np.asarray(inputs["L0_attn_bv"]).reshape(16, 128)),
        "bo": np.ascontiguousarray(np.asarray(inputs["L0_attn_bo"]).reshape(1, D)),
        "wg": np.asarray(inputs["L0_ffn_w_gate"]),
        "wu": np.asarray(inputs["L0_ffn_w_up"]),
        "wd": np.asarray(inputs["L0_ffn_w_down"]),
        "sg": np.ascontiguousarray(np.asarray(inputs["L0_ffn_s_gate"]).reshape(64, 128)),
        "su": np.ascontiguousarray(np.asarray(inputs["L0_ffn_s_up"]).reshape(64, 128)),
        "sd": np.ascontiguousarray(np.asarray(inputs["L0_ffn_s_down"]).reshape(1, D)),
    }
    in_maps = []
    for c in range(8):
        b, h = c // 2, c % 2
        m = dict(base)
        m["x"] = np.ascontiguousarray(x[b, h * R:(h + 1) * R, :])
        m["msk"] = np.full((128, 1), float(h), dtype=np.float32)
        in_maps.append(m)

    res = run_bass_kernel_spmd(nc, in_maps, core_ids=list(range(8)))
    out = np.empty((4, 2048, D), dtype=np.float32)
    for c in range(8):
        b, h = c // 2, c % 2
        out[b, h * R:(h + 1) * R, :] = res.results[c]["y"]
    return out


# version 09
# speedup vs baseline: 1.0391x; 1.0391x over previous
import numpy as np
import concourse.bass as bass
import concourse.mybir as mybir
from concourse.bass_utils import run_bass_kernel_spmd

F32 = mybir.dt.float32
BF16 = mybir.dt.bfloat16
I8 = mybir.dt.int8
ALU = mybir.AluOpType
AF = mybir.ActivationFunctionType
AXX = mybir.AxisListType.X

D = 2048
INTER = 8192
R = 1024
EPS = 1e-5
SCALE = 128.0 ** -0.5

ENGS = ("g", "y", "t", "v", "s")


class Plan:

    def __init__(self):
        self.steps = {e: [] for e in ENGS}
        self.cnt = {}

    def tok(self, sem):
        return (sem, self.cnt.get(sem, 0))

    def emit(self, eng, fn, waits=(), inc=None, amt=1):
        ws = [w for w in waits if w is not None and w[1] > 0]
        for s, v in ws:
            assert v <= self.cnt.get(s, 0), f"wait on future {s}={v} (now {self.cnt.get(s,0)})"
        tok = None
        if inc is not None:
            self.cnt[inc] = self.cnt.get(inc, 0) + amt
            tok = (inc, self.cnt[inc])
        self.steps[eng].append((fn, ws, inc, amt))
        return tok

    def dma(self, eng, sem, out, in_, waits=(), **kw):
        return self.emit(eng, lambda e: e.dma_start(out, in_, **kw), waits, sem, 16)


def build_program(dbg=False, stop=None):
    nc = bass.Bass("TRN2", target_bir_lowering=False, detect_race_conditions=False)

    xin = nc.dram_tensor("x", [R, D], F32, kind="ExternalInput")
    wq = nc.dram_tensor("wq", [D, D], F32, kind="ExternalInput")
    wk = nc.dram_tensor("wk", [D, D], F32, kind="ExternalInput")
    wv = nc.dram_tensor("wv", [D, D], F32, kind="ExternalInput")
    wo = nc.dram_tensor("wo", [D, D], F32, kind="ExternalInput")
    bq = nc.dram_tensor("bq", [16, 128], F32, kind="ExternalInput")
    bk = nc.dram_tensor("bk", [16, 128], F32, kind="ExternalInput")
    bv = nc.dram_tensor("bv", [16, 128], F32, kind="ExternalInput")
    bo = nc.dram_tensor("bo", [1, D], F32, kind="ExternalInput")
    wg = nc.dram_tensor("wg", [D, INTER], I8, kind="ExternalInput")
    wu = nc.dram_tensor("wu", [D, INTER], I8, kind="ExternalInput")
    wd = nc.dram_tensor("wd", [INTER, D], I8, kind="ExternalInput")
    sg = nc.dram_tensor("sg", [64, 128], F32, kind="ExternalInput")
    su = nc.dram_tensor("su", [64, 128], F32, kind="ExternalInput")
    sd = nc.dram_tensor("sd", [1, D], F32, kind="ExternalInput")
    msk_in = nc.dram_tensor("msk", [128, 1], F32, kind="ExternalInput")
    y = nc.dram_tensor("y", [R, D], F32, kind="ExternalOutput")

    w_bf = {
        "q": nc.dram_tensor("wq_bf", [D, D], BF16),
        "k": nc.dram_tensor("wk_bf", [D, D], BF16),
        "v": nc.dram_tensor("wv_bf", [D, D], BF16),
        "o": nc.dram_tensor("wo_bf", [D, D], BF16),
    }
    bo_bf_d = nc.dram_tensor("bo_bf_d", [1, D], BF16)
    if dbg:
        dbg_xnt = nc.dram_tensor("dbg_xnt", [128, 16 * R], BF16, kind="ExternalOutput")
        dbg_pkv = nc.dram_tensor("dbg_pkv", [128, 16 * R], BF16, kind="ExternalOutput")
        dbg_tot = nc.dram_tensor("dbg_tot", [128, 16], F32, kind="ExternalOutput")
        dbg_ini = nc.dram_tensor("dbg_ini", [128, 16], F32, kind="ExternalOutput")
        dbg_ctx = nc.dram_tensor("dbg_ctx", [128, 16 * R], BF16, kind="ExternalOutput")
        dbg_xn2 = nc.dram_tensor("dbg_xn2", [128, 16 * R], BF16, kind="ExternalOutput")
        dbg_x2 = nc.dram_tensor("dbg_x2", [R, D], F32, kind="ExternalOutput")
        dbg_att = nc.dram_tensor("dbg_att", [128, 4096], F32, kind="ExternalOutput")
    cc_in = nc.dram_tensor("cc_in", [128, 16], F32)
    cc_out = nc.dram_tensor("cc_out", [256, 16], F32)

    plan = Plan()
    E = plan.emit
    DMA = plan.dma
    TOK = plan.tok

    from contextlib import ExitStack
    with ExitStack() as stack:
        en = stack.enter_context
        resid = en(nc.sbuf_tensor("resid", [128, 8 * D], F32))
        xnt = en(nc.sbuf_tensor("xnt", [128, 16 * R], BF16))
        pkv = en(nc.sbuf_tensor("pkv", [128, 16 * R], BF16))
        wtb = en(nc.sbuf_tensor("wtb", [128, 16384], BF16))
        a5 = en(nc.sbuf_tensor("a5", [128, 4096], F32))
        scr = en(nc.sbuf_tensor("scr", [128, 2048], F32))
        identb = en(nc.sbuf_tensor("identb", [128, 128], BF16))
        identf = en(nc.sbuf_tensor("identf", [128, 128], F32))
        onesb = en(nc.sbuf_tensor("onesb", [1, 128], BF16))
        bcol = en(nc.sbuf_tensor("bcol", [128, 48], F32))
        bo_hi = en(nc.sbuf_tensor("bo_hi", [1, D], BF16))
        sgcol = en(nc.sbuf_tensor("sgcol", [128, 64], F32))
        sucol = en(nc.sbuf_tensor("sucol", [128, 64], F32))
        msk = en(nc.sbuf_tensor("msk_sb", [128, 1], F32))
        totals = en(nc.sbuf_tensor("totals", [128, 16], F32))
        initc = en(nc.sbuf_tensor("initc", [128, 16], F32))
        stats = en(nc.sbuf_tensor("stats", [128, 64], F32))
        stage1 = en(nc.sbuf_tensor("stage1", [64, 128], F32))
        dly_v = en(nc.sbuf_tensor("dly_v", [128, 560], F32))
        wnat = en(nc.sbuf_tensor("wnat", [128, 4096], BF16))
        dly_s = en(nc.sbuf_tensor("dly_s", [128, 512], F32))
        psA = en(nc.psum_tensor("psA", [128, 1024], F32))
        psB = en(nc.psum_tensor("psB", [128, 1024], F32))
        psS = en(nc.psum_tensor("psS", [128, 1024], F32))
        psT = en(nc.psum_tensor("psT", [128, 2048], BF16))
        psX = psS
        sem_names = ["t", "v", "s", "g", "cc",
                     "dx", "dck", "dcv", "dcq", "dco", "dcb", "dmisc", "dstg",
                     "dw0", "dw1", "dcc", "dst", "dg0", "dg1", "dg2",
                     "dd0", "dd1", "dsd", "dr0", "dr1", "df0", "df1"]
        sems = {nm: en(nc.semaphore("sem_" + nm)) for nm in sem_names}
        block = en(nc.Block())

        a5b = a5.bitcast(BF16)
        fuse = resid.bitcast(BF16)
        otile = pkv.bitcast(F32)
        scrb = scr.bitcast(BF16)
        wtbf = wtb.bitcast(F32)
        assert list(a5b.shape) == [128, 8192], a5b.shape

        tg1 = E("g", lambda e: e.memset(identb[:], 0.0), inc="g")
        tg2 = E("g", lambda e: e.memset(identf[:], 0.0), inc="g")
        tg3 = E("g", lambda e: e.affine_select(out=identb[:], in_=identb[:],
                                               compare_op=ALU.not_equal, fill=1.0, base=0,
                                               pattern=[[-1, 128]], channel_multiplier=1),
                waits=[tg1], inc="g")
        tg4 = E("g", lambda e: e.affine_select(out=identf[:], in_=identf[:],
                                               compare_op=ALU.not_equal, fill=1.0, base=0,
                                               pattern=[[-1, 128]], channel_multiplier=1),
                waits=[tg2], inc="g")
        t_ident = E("g", lambda e: e.memset(onesb[:], 1.0), waits=[tg3, tg4], inc="g")
        E("v", lambda e: e.memset(dly_v[:], 0.0))
        E("s", lambda e: e.memzero(dly_s[:]))
        E("v", lambda e: e.memset(stats[:, 58:59], 1.0 / D))
        E("v", lambda e: e.memset(stats[:, 59:60], -1.0))
        E("v", lambda e: e.memset(stats[:, 60:61], EPS))
        E("v", lambda e: e.memset(stats[:, 61:62], -0.5))
        t_consts = E("v", lambda e: e.tensor_add(dly_v[:], dly_v[:], dly_v[:]), inc="v")

        for rt in range(8):
            DMA("g", "dx", resid[:, rt * D:(rt + 1) * D],
                xin[rt * 128:(rt + 1) * 128, :])
        t_x = TOK("dx")
        wsrc_of = {"k": wk, "v": wv, "q": wq, "o": wo}
        t_bo_cast = DMA("g", "dcb", bo_bf_d[:, :], bo[:, :])

        DMA("y", "dmisc", msk[:], msk_in[:, :])
        t_msk = TOK("dmisc")

        prep_done = []
        stage_free = None
        for src, n, dst in ((bq, 16, bcol[:, 0:16]), (bk, 16, bcol[:, 16:32]),
                            (bv, 16, bcol[:, 32:48]), (sg, 64, sgcol[:, 0:64]),
                            (su, 64, sucol[:, 0:64])):
            tl = DMA("y", "dstg", stage1[0:n, :], src[0:n, :], waits=[stage_free])
            tt = E("t", lambda e, n_=n: e.transpose(psX[0:128, 0:n_], stage1[0:n_, :],
                                                    identf[0:n_, 0:n_]),
                   waits=[tl, t_ident, TOK("s")], inc="t")
            E("s", lambda e, n_=n, d_=dst: e.copy(d_, psX[:, 0:n_]), waits=[tt])
            ts_ = E("s", lambda e: e.copy(dly_s[:], dly_s[:]), inc="s")
            prep_done.append(ts_)
            stage_free = ts_
        prep_q, prep_k, prep_v, prep_sg, prep_su = prep_done

        def stats_chain(c, width, wv_tok):
            cw = slice(0, 0)
            def a_(out_c, in_c, func, bias=None, scale=None, w=width):
                def fn(e, oc=out_c, ic=in_c, f=func, b=bias, sc=scale, w_=w):
                    kw = {}
                    if b is not None:
                        kw["bias"] = stats[:, b:b + 1] if isinstance(b, int) else b
                    if sc is not None:
                        kw["scale"] = stats[:, sc:sc + 1] if isinstance(sc, int) else sc
                    return e.activation(stats[:, oc:oc + w_], stats[:, ic:ic + w_], f, **kw)
                return fn
            E("s", a_(24 + c, c, AF.Identity, scale=stats[:, 58:59]),
              waits=[wv_tok, t_consts])
            E("s", a_(16 + c, 8 + c, AF.Identity, scale=stats[:, 58:59]))
            E("s", a_(8 + c, 24 + c, AF.Square))
            E("s", a_(8 + c, 8 + c, AF.Identity, scale=stats[:, 59:60]))
            for j in range(width):
                E("s", a_(32 + c + j, 16 + c + j, AF.Identity, bias=8 + c + j, w=1))
            E("s", a_(32 + c, 32 + c, AF.Ln, bias=stats[:, 60:61]))
            E("s", a_(32 + c, 32 + c, AF.Exp, scale=stats[:, 61:62]))
            E("s", a_(48 + c, 24 + c, AF.Identity, scale=stats[:, 59:60]))
            for j in range(width):
                E("s", a_(40 + c + j, 48 + c + j, AF.Identity, scale=32 + c + j, w=1))
            return E("s", lambda e: e.copy(dly_s[:], dly_s[:]), inc="s")

        scr_free = [("v", 0)]

        def ln_sums(c, src_ap, first_waits):
            E("v", lambda e: e.reduce_sum(stats[:, c:c + 1], src_ap, axis=AXX),
              waits=list(first_waits))
            E("s", lambda e: e.activation(scr[:, 0:1024], src_ap[:, 0:1024], AF.Square),
              waits=list(first_waits) + [scr_free[0]])
            sqb = E("s", lambda e: e.activation(scr[:, 1024:2048], src_ap[:, 1024:2048],
                                                AF.Square), inc="s")
            E("v", lambda e: e.reduce_sum(stats[:, 8 + c:9 + c], scr[:], axis=AXX),
              waits=[sqb])
            tok = E("v", lambda e: e.tensor_add(dly_v[:], dly_v[:], dly_v[:]), inc="v")
            scr_free[0] = tok
            return tok

        def ln_stats(c, src_ap, first_waits):
            return stats_chain(c, 1, ln_sums(c, src_ap, first_waits))

        def transpose_16(src_bf, rt, first_wait, psT_free):
            fw = list(first_wait) if isinstance(first_wait, list) else [first_wait]
            last_ev = None
            for b4 in range(4):
                sl = b4 % 2
                tt = None
                for j in range(4):
                    m = b4 * 4 + j
                    tt = E("t", lambda e, i_=src_bf[:, m * 128:(m + 1) * 128],
                           o_=psT[:, sl * 1024 + j * 128:sl * 1024 + (j + 1) * 128]:
                           e.transpose(o_, i_, identb[:]),
                           waits=fw + [psT_free[sl], t_ident],
                           inc="t" if j == 3 else None)
                ev = E("v", lambda e, m0=b4 * 4, r0=rt * 128, sl_=sl: e.tensor_copy(
                    bass.AP(xnt, m0 * R + r0, [[16 * R, 128], [R, 4], [1, 128]]),
                    bass.AP(psT, sl_ * 1024, [[2048, 128], [128, 4], [1, 128]])),
                    waits=[tt], inc="v")
                psT_free[sl] = ev
                last_ev = ev
            return last_ev

        for rt in range(4):
            ln_sums(rt, resid[:, rt * D:(rt + 1) * D], [t_x])
        t_stats1a = stats_chain(0, 4, TOK("v"))
        for rt in range(4, 8):
            ln_sums(rt, resid[:, rt * D:(rt + 1) * D], [t_x])
        t_stats1b = stats_chain(4, 4, TOK("v"))

        psT_free = [TOK("v"), TOK("v")]
        xn_free = [TOK("t"), TOK("t")]
        xnt_rh = [None, None]
        for rt in range(8):
            par = rt % 2
            xnb = a5b[:, par * 2048:(par + 1) * 2048]
            tn = E("s", lambda e, rt_=rt, o_=xnb: e.activation(
                o_, resid[:, rt_ * D:(rt_ + 1) * D], AF.Identity,
                bias=stats[:, 40 + rt_:41 + rt_], scale=stats[:, 32 + rt_:33 + rt_]),
                waits=[t_stats1a if rt < 4 else t_stats1b, xn_free[par]], inc="s")
            ev = transpose_16(xnb, rt, tn, psT_free)
            xn_free[par] = TOK("t")
            if rt == 3:
                xnt_rh[0] = ev
            if rt == 7:
                xnt_rh[1] = ev
        bar_p1 = [TOK("t"), TOK("v"), TOK("s")]
        if dbg:
            bar_p1.append(DMA("y", "dmisc", dbg_xnt[:, :], xnt[:], waits=bar_p1))

        _lvl = {"p1": 1, "p2": 2, "p3": 3, None: 9}[stop]
        if dbg and _lvl == 1:
            for rt_ in range(8):
                DMA("y", "dmisc", dbg_x2[rt_ * 128:(rt_ + 1) * 128, :],
                    resid[:, rt_ * D:(rt_ + 1) * D], waits=bar_p1)
            DMA("y", "dmisc", dbg_tot[:, :], stats[:, 0:16], waits=bar_p1)
            DMA("y", "dmisc", dbg_ini[:, :], stats[:, 24:40], waits=bar_p1)
            DMA("y", "dmisc", dbg_ctx[:, 0:32], stats[:, 48:64].bitcast(BF16), waits=bar_p1)
            DMA("y", "dmisc", dbg_pkv[:, 0:4096], a5b[:, 0:4096], waits=bar_p1)
        if _lvl >= 2:
            wslot_free = [("t", 0), ("t", 0)]
            wnat_free = [("t", 0), ("t", 0)]

            def load_wt_group_pe(key, g, p):
                wsrc = wsrc_of[key]
                ev = None
                for ib in range(4):
                    eb = g * 4 + ib
                    ns = eb % 2
                    ld = DMA("g", "dg%d" % ns,
                             wnat[:, ns * 2048:(ns + 1) * 2048],
                             wsrc[eb * 128:(eb + 1) * 128, :],
                             waits=[wnat_free[ns]])
                    for tb in range(4):
                        psl = tb % 2
                        tt = None
                        for j in range(4):
                            t_ = tb * 4 + j
                            tt = E("t", lambda e, ns_=ns, t2=t_, psl_=psl, j_=j:
                                   e.transpose(
                                       psT[:, psl_ * 1024 + j_ * 128:
                                           psl_ * 1024 + (j_ + 1) * 128],
                                       wnat[:, ns_ * 2048 + t2 * 128:
                                            ns_ * 2048 + (t2 + 1) * 128],
                                       identb[:]),
                                   waits=[ld, psT_free[psl], t_ident],
                                   inc="t" if j == 3 else None)
                        ev = E("v", lambda e, p_=p, tb_=tb, ib_=ib, psl_=psl:
                               e.tensor_copy(
                                   bass.AP(wtb, p_ * 8192 + tb_ * 4 * 512 + ib_ * 128,
                                           [[16384, 128], [512, 4], [1, 128]]),
                                   bass.AP(psT, psl_ * 1024,
                                           [[2048, 128], [128, 4], [1, 128]])),
                               waits=[tt, wslot_free[p] if (ib == 0 and tb == 0) else None],
                               inc="v")
                        psT_free[psl] = ev
                    wnat_free[ns] = tt
                return ev

            def load_wt_group(wbf_dram, g, extra_wait):
                p = g % 2
                first = True
                tok = None
                for d in range(16):
                    tok = DMA("y", "dw%d" % p,
                              wtb[:, p * 8192 + d * 512:p * 8192 + d * 512 + 512],
                              wbf_dram[g * 512:(g + 1) * 512, d * 128:(d + 1) * 128],
                              waits=[wslot_free[p], extra_wait] if first else (),
                              transpose=True)
                    first = False
                return tok

            def proj_mms(m, ps, extra_waits):
                p = (m // 4) % 2
                last = None
                for rh in range(2):
                    for d in range(16):
                        if d == 0:
                            ws = ([xnt_rh[rh]] + (list(extra_waits) if rh == 0 else []))
                        else:
                            ws = ()
                        last = E("t", lambda e, d_=d, rh_=rh, m_=m, p_=p, ps_=ps: e.matmul(
                            ps_[:, rh_ * 512:(rh_ + 1) * 512],
                            wtb[:, p_ * 8192 + d_ * 512 + (m_ % 4) * 128:
                                p_ * 8192 + d_ * 512 + (m_ % 4) * 128 + 128],
                            xnt[:, d_ * R + rh_ * 512:d_ * R + rh_ * 512 + 512],
                            start=(d_ == 0), stop=(d_ == 15)),
                            waits=ws,
                            inc="t" if (rh == 1 and d == 15) else None)
                return last

            ps_free = {0: ("v", 0), 1: ("v", 0)}
            TEXP = [a5b[:, 0:1024], a5b[:, 1024:2048]]
            TRELU = a5b[:, 2048:3072]
            TV = [a5b[:, 3072:4096], a5b[:, 4096:5120]]
            PHIQ = a5b[:, 5120:6144]
            texp_free = [TOK("v"), TOK("v")]
            tv_free = [TOK("v"), TOK("v")]

            def wt_pass(key, evac, first_extra, preloaded=None):
                loaded = preloaded if preloaded is not None else {
                    0: load_wt_group_pe(key, 0, 0),
                    1: load_wt_group_pe(key, 1, 1)}
                for m in range(16):
                    g = m // 4
                    ps = psA if m % 2 == 0 else psB
                    extra = [loaded[g], ps_free[m % 2]] + (first_extra if m == 0 else [])
                    tmm = proj_mms(m, ps, extra)
                    if m % 4 == 3:
                        wslot_free[g % 2] = tmm
                        if g + 2 < 4:
                            loaded[g + 2] = load_wt_group_pe(key, g + 2, g % 2)
                    evac(m, ps, tmm)

            def evac_k(m, ps, tmm):
                par = m % 2
                te = E("s", lambda e, ps_=ps, m_=m, p_=par: e.activation(
                    TEXP[p_], ps_[:], AF.Exp, bias=bcol[:, 16 + m_:17 + m_]),
                    waits=[tmm, texp_free[par], prep_k], inc="s")
                tr = E("v", lambda e, ps_=ps, m_=m: e.tensor_scalar(
                    TRELU, ps_[:], bcol[:, 16 + m_:17 + m_], 0.0, ALU.add, ALU.max),
                    waits=[te], inc="v")
                ps_free[m % 2] = tr
                ph = E("v", lambda e, m_=m, p_=par: e.scalar_tensor_tensor(
                    pkv[:, m_ * R:(m_ + 1) * R], TEXP[p_], 1.0, TRELU, ALU.min, ALU.add),
                    inc="v")
                texp_free[par] = ph

            wt_pass("k", evac_k, [])

            tot_toks = []

            def evac_v(m, ps, tmm):
                par = m % 2
                tv_ = E("s", lambda e, ps_=ps, m_=m, p_=par: e.activation(
                    TV[p_], ps_[:], AF.Identity, bias=bcol[:, 32 + m_:33 + m_]),
                    waits=[tmm, tv_free[par], prep_v], inc="s")
                ps_free[m % 2] = tv_
                mu = E("v", lambda e, m_=m, p_=par: e.tensor_mul(
                    pkv[:, m_ * R:(m_ + 1) * R], pkv[:, m_ * R:(m_ + 1) * R], TV[p_]),
                    waits=[tv_], inc="v")
                tv_free[par] = mu
                E("v", lambda e, m_=m: e.reduce_sum(
                    totals[:, m_:m_ + 1], pkv[:, m_ * R:(m_ + 1) * R], axis=AXX))
                if m == 15:
                    tot_toks.append(E("v", lambda e: e.tensor_add(dly_v[:], dly_v[:], dly_v[:]),
                                     inc="v"))
                else:
                    tot_toks.append(TOK("v"))

            wt_pass("v", evac_v, [])

            q_preload = {0: load_wt_group_pe("q", 0, 0),
                         1: load_wt_group_pe("q", 1, 1)}

            if dbg:
                dtok = DMA("y", "dmisc", dbg_pkv[:, :], pkv[:], waits=[tot_toks[-1]])
                DMA("y", "dmisc", dbg_tot[:, :], totals[:], waits=[tot_toks[-1]])

            d1 = DMA("y", "dcc", cc_in[:, :], totals[:, 0:16], waits=[tot_toks[-1]])
            ccd = E("g", lambda e: e.collective_compute(
                "AllGather", ALU.bypass, replica_groups=[[0, 1], [2, 3], [4, 5], [6, 7]],
                ins=[cc_in.ap().opt()], outs=[cc_out.ap().opt()]),
                waits=[d1], inc="cc")
            d2 = DMA("y", "dcc", initc[:, 0:16], cc_out[0:128, :], waits=[ccd])
            E("s", lambda e: e.activation(initc[:, 0:16], initc[:, 0:16], AF.Identity,
                                          scale=msk[:]),
              waits=[d2, t_msk])
            t_init = E("s", lambda e: e.copy(dly_s[:], dly_s[:]), inc="s")
            t_bo = DMA("y", "dmisc", bo_hi[:], bo_bf_d[:, :], waits=[t_bo_cast])
            if dbg:
                DMA("y", "dmisc", dbg_ini[:, :], initc[:], waits=[t_init])

            ctx_last = [None]

            def evac_q(m, ps, tmm):
                par = m % 2
                te = E("s", lambda e, ps_=ps, m_=m, p_=par: e.activation(
                    TEXP[p_], ps_[:], AF.Exp, bias=bcol[:, m_:1 + m_]),
                    waits=[tmm, texp_free[par], prep_q], inc="s")
                tr = E("v", lambda e, ps_=ps, m_=m: e.tensor_scalar(
                    TRELU, ps_[:], bcol[:, m_:1 + m_], 0.0, ALU.add, ALU.max),
                    waits=[te], inc="v")
                ps_free[m % 2] = tr
                E("v", lambda e, p_=par: e.scalar_tensor_tensor(
                    PHIQ, TEXP[p_], 1.0, TRELU, ALU.min, ALU.add))
                E("v", lambda e, m_=m: e.tensor_tensor_scan(
                    psS[:], pkv[:, m_ * R:(m_ + 1) * R], pkv[:, m_ * R:(m_ + 1) * R],
                    initc[:, m_:m_ + 1], ALU.add, ALU.bypass),
                    waits=[t_init] + ([dtok] if dbg else []))
                cx = E("v", lambda e, m_=m: e.scalar_tensor_tensor(
                    pkv[:, m_ * R:(m_ + 1) * R], psS[:], SCALE, PHIQ, ALU.mult, ALU.mult),
                    inc="v")
                texp_free[par] = cx
                ctx_last[0] = cx

            wt_pass("q", evac_q, [], preloaded=q_preload)
            bar_ctx = ctx_last[0]


        if _lvl >= 3:
            PS3 = [psA[:, 0:512], psA[:, 512:1024], psB[:, 0:512], psB[:, 512:1024]]
            ps3_free = {i: TOK("v") for i in range(4)}
            x2_store = []
            tnorm_toks = []
            attn_bf = xnt

            wo_loaded = {0: load_wt_group_pe("o", 0, 0),
                         1: load_wt_group_pe("o", 1, 1)}
            evac_toks = {rt: [] for rt in range(8)}
            for g in range(4):
                p = g % 2
                last = None
                for rt in range(8):
                    slot = rt % 4
                    for d in range(16):
                        E("t", lambda e, d_=d, rt_=rt, p_=p, sl=slot: e.matmul(
                            PS3[sl],
                            pkv[:, d_ * R + rt_ * 128:d_ * R + rt_ * 128 + 128],
                            wtb[:, p_ * 8192 + d_ * 512:p_ * 8192 + d_ * 512 + 512],
                            start=(d_ == 0), stop=False),
                            waits=[wo_loaded[g], ps3_free[slot], bar_ctx, t_bo]
                            if d == 0 else ())
                    last = E("t", lambda e, g_=g, sl=slot: e.matmul(
                        PS3[sl], onesb[0:1, :], bo_hi[0:1, g_ * 512:(g_ + 1) * 512],
                        start=False, stop=True), inc="t")
                    ev = E("s", lambda e, g_=g, rt_=rt, sl=slot: e.copy(
                        attn_bf[:, rt_ * 2048 + g_ * 512:rt_ * 2048 + g_ * 512 + 512],
                        PS3[sl]), waits=[last], inc="s")
                    ps3_free[slot] = ev
                    evac_toks[rt].append(ev)
                wslot_free[p] = last
                if g + 2 < 4:
                    wo_loaded[g + 2] = load_wt_group_pe("o", g + 2, g % 2)

            if dbg:
                datt = DMA("y", "dmisc", dbg_att[:, :],
                           attn_bf[:, 0:4096], waits=evac_toks[0] + evac_toks[1])

            for rt in range(8):
                c = rt % 2
                at = attn_bf[:, rt * 2048:(rt + 1) * 2048]
                tn2 = ln_stats(c, at, evac_toks[rt] +
                               ([datt] if (dbg and rt < 2) else []))
                E("v", lambda e, a_=at, c_=c: e.tensor_scalar(
                    scr[:], a_, stats[:, 48 + c_:49 + c_], stats[:, 32 + c_:33 + c_],
                    ALU.add, ALU.mult), waits=[tn2])
                add_t = E("v", lambda e, rt_=rt: e.tensor_add(
                    resid[:, rt_ * D:(rt_ + 1) * D], resid[:, rt_ * D:(rt_ + 1) * D],
                    scr[:]), inc="v")
                scr_free[0] = add_t
                x2_store.append(DMA("y", "dst", y[rt * 128:(rt + 1) * 128, :],
                                    resid[:, rt * D:(rt + 1) * D], waits=[add_t]))
                tn3 = ln_stats(2 + c, resid[:, rt * D:(rt + 1) * D], [add_t])
                xn2b = fuse[:, rt * 4096:rt * 4096 + 2048]
                tnorm = E("s", lambda e, rt_=rt, o_=xn2b, c_=2 + c: e.activation(
                    o_, resid[:, rt_ * D:(rt_ + 1) * D], AF.Identity,
                    bias=stats[:, 40 + c_:41 + c_], scale=stats[:, 32 + c_:33 + c_]),
                    waits=[tn3, x2_store[-1]], inc="s")
                tnorm_toks.append(tnorm)

            t_applies = TOK("v")
            xn2T_done = []
            xnt2_rh = [None, None]
            for rt in range(8):
                xn2T_done.append(transpose_16(fuse[:, rt * 4096:rt * 4096 + 2048], rt,
                                              [tnorm_toks[rt], t_applies], psT_free))
            xnt2_rh[0] = xn2T_done[3]
            xnt2_rh[1] = xn2T_done[7]

            bar_p3 = [TOK("t"), TOK("v"), TOK("s"), x2_store[-1]]
            if dbg:
                bar_p3.append(DMA("y", "dmisc", dbg_ctx[:, :], pkv[:], waits=[bar_ctx]))
                bar_p3.append(DMA("y", "dmisc", dbg_xn2[:, :], xnt[:], waits=bar_p3[:4]))
                for rt_ in range(8):
                    tok_ = DMA("y", "dmisc", dbg_x2[rt_ * 128:(rt_ + 1) * 128, :],
                               resid[:, rt_ * D:(rt_ + 1) * D], waits=bar_p3[:4])
                bar_p3.append(tok_)


        if _lvl >= 9:
            GU = [(wtb, 0), (wtb, 8192), (a5b, 0)]
            TG = [scrb[:, 0:512], scrb[:, 512:1024]]
            TU = [scrb[:, 1024:1536], scrb[:, 1536:2048]]

            def load_gu(wdram, s_, slot, wait_tok):
                parent, base = GU[slot]
                ncols = parent.shape[1]
                first = True
                tok = None
                for t2 in range(2):
                    tok = DMA("g", "dg%d" % slot,
                              bass.AP(parent, base + t2 * 8 * 512,
                                      [[ncols, 128], [512, 8], [1, 512]]),
                              bass.AP(wdram, t2 * 8 * 128 * INTER + s_ * 512,
                                      [[INTER, 128], [128 * INTER, 8], [1, 512]]),
                              waits=[wait_tok] if first else ())
                    first = False
                return tok

            def gu_slice(slot, d, icw):
                parent, base = GU[slot]
                return parent[:, base + d * 512 + icw * 128: base + d * 512 + icw * 128 + 128]

            bar_h = bar_p3
            h_slotC = bar_p3[0]
    
            x2l_free = [None, None]
            wd_last_prev = {0: bar_p3[0], 1: bar_p3[0]}
            psd_prev = None
            for h in range(2):
                if h == 0:
                    slot_last = {0: wslot_free[0], 1: wslot_free[1], 2: bar_ctx}
                else:
                    slot_last = {0: wd_last_prev[0], 1: wd_last_prev[1], 2: h_slotC}
                loaded = {(0, 0): load_gu(wg, 0, 0, slot_last[0]),
                          (0, 1): load_gu(wu, 0, 1, slot_last[1])}
                slot_of = {(0, 0): 0, (0, 1): 1}
                for s_ in range(1, 16):
                    slot_of[(s_, 0)] = (2 * s_) % 3
                    slot_of[(s_, 1)] = (2 * s_ + 1) % 3
                tg_free = [TOK("v"), TOK("v")]
                tu_free = [TOK("v"), TOK("v")]
                if psd_prev is None:
                    psg_free = [TOK("s"), TOK("s")]
                    psu_free = [TOK("v"), TOK("v")]
                else:
                    psg_free = [psd_prev[0], psd_prev[2]]
                    psu_free = [psd_prev[1], psd_prev[3]]
                fuse_last = [None]
                for s_ in range(16):
                    slg, slu = slot_of[(s_, 0)], slot_of[(s_, 1)]
                    lastg = lastu = None
                    for icw in range(4):
                        ic = s_ * 4 + icw
                        par = ic % 2
                        psg = (psA if par == 0 else psB)[:, 0:512]
                        psu = (psA if par == 0 else psB)[:, 512:1024]
                        for d in range(16):
                            lastg = E("t", lambda e, d_=d, sl=slg, ic_=icw, ps_=psg, h_=h: e.matmul(
                                ps_, gu_slice(sl, d_, ic_),
                                xnt[:, d_ * R + h_ * 512:d_ * R + h_ * 512 + 512],
                                start=(d_ == 0), stop=(d_ == 15)),
                                waits=([loaded[(s_, 0)], psg_free[par]] +
                                       ([xnt2_rh[h]] if (s_ == 0 and icw == 0) else []))
                                if d == 0 else (),
                                inc="t" if d == 15 else None)
                        for d in range(16):
                            lastu = E("t", lambda e, d_=d, sl=slu, ic_=icw, ps_=psu, h_=h: e.matmul(
                                ps_, gu_slice(sl, d_, ic_),
                                xnt[:, d_ * R + h_ * 512:d_ * R + h_ * 512 + 512],
                                start=(d_ == 0), stop=(d_ == 15)),
                                waits=[loaded[(s_, 1)], psu_free[par]] if d == 0 else (),
                                inc="t" if d == 15 else None)
                        tg_t = E("s", lambda e, ps_=psg, ic_=ic, p_=par: e.activation(
                            TG[p_], ps_, AF.Silu, scale=sgcol[:, ic_:ic_ + 1]),
                            waits=[lastg, tg_free[par], prep_sg], inc="s")
                        psg_free[par] = tg_t
                        tu_t = E("v", lambda e, ps_=psu, ic_=ic, p_=par: e.tensor_scalar(
                            TU[p_], ps_, sucol[:, ic_:ic_ + 1], None, ALU.mult),
                            waits=[lastu, tu_free[par], prep_su], inc="v")
                        psu_free[par] = tu_t
                        f_t = E("v", lambda e, ic_=ic, p_=par: e.tensor_mul(
                            fuse[:, ic_ * 512:(ic_ + 1) * 512], TG[p_], TU[p_]),
                            waits=[tg_t], inc="v")
                        tg_free[par] = f_t
                        tu_free[par] = f_t
                        fuse_last[0] = f_t
                    slot_last[slg] = lastg
                    slot_last[slu] = lastu
                    if s_ + 1 < 16:
                        ng, nu = slot_of[(s_ + 1, 0)], slot_of[(s_ + 1, 1)]
                        loaded[(s_ + 1, 0)] = load_gu(wg, s_ + 1, ng, slot_last[ng])
                        loaded[(s_ + 1, 1)] = load_gu(wu, s_ + 1, nu, slot_last[nu])
                bar_4a = [TOK("t"), TOK("v"), TOK("s")]

                sdb = wtbf[:, 2048:4096]
                t_sdb = DMA("g", "dsd", sdb, bass.AP(sd, 0, [[0, 128], [1, 2048]]),
                            waits=[bar_4a[0]])
                WDSL = [(0, None), (2048, None)]

                def load_wd(s_, dgi, wait_tok):
                    p = s_ % 2
                    return DMA("g", "dd%d" % p,
                               bass.AP(wtb, WDSL[p][0], [[16384, 128], [512, 4], [1, 512]]),
                               bass.AP(wd, s_ * 512 * D + dgi * 512,
                                       [[D, 128], [128 * D, 4], [1, 512]]),
                               waits=[wait_tok])

                PSD = [psA[:, 0:512], psA[:, 512:1024], psB[:, 0:512], psB[:, 512:1024]]
                psd_free = {i: bar_4a[1] for i in range(4)}
                wd_last = {0: bar_4a[0], 1: bar_4a[0]}
                ffn_evac = {rs: [] for rs in range(4)}
                for dgi in range(4):
                    wloaded = {0: load_wd(0, dgi, wd_last[0]),
                               1: load_wd(1, dgi, wd_last[1])}
                    group_end = {}
                    for s_ in range(16):
                        p = s_ % 2
                        for icw in range(4):
                            ic = s_ * 4 + icw
                            for rs in range(4):
                                first = (s_ == 0 and icw == 0)
                                lastmm = (s_ == 15 and icw == 3)
                                if first:
                                    ws = [wloaded[p], psd_free[rs]] + \
                                         (bar_4a + [fuse_last[0]] if (rs == 0 and dgi == 0) else [])
                                elif icw == 0 and rs == 0:
                                    ws = [wloaded[p]]
                                else:
                                    ws = ()
                                tok = E("t", lambda e, ic_=ic, rs_=rs, p_=p, ic2=icw,
                                        fl=first, ll=lastmm: e.matmul(
                                    PSD[rs_],
                                    fuse[:, ic_ * 512 + rs_ * 128:ic_ * 512 + rs_ * 128 + 128],
                                    wtb[:, WDSL[p_][0] + ic2 * 512:WDSL[p_][0] + ic2 * 512 + 512],
                                    start=fl, stop=ll),
                                    waits=ws,
                                    inc="t" if (icw == 3 and rs == 3) else None)
                                if icw == 3 and rs == 3:
                                    group_end[s_] = tok
                        wd_last[p] = group_end[s_]
                        if s_ + 2 < 16:
                            wloaded[p] = load_wd(s_ + 2, dgi, wd_last[p])
                    for rs in range(4):
                        ev = E("v", lambda e, rs_=rs, dg_=dgi: e.scalar_tensor_tensor(
                            otile[:, rs_ * 2048 + dg_ * 512:rs_ * 2048 + dg_ * 512 + 512],
                            PSD[rs_], 1.0, sdb[:, dg_ * 512:(dg_ + 1) * 512],
                            ALU.bypass, ALU.mult),
                            waits=[group_end[15], t_sdb], inc="v")
                        ffn_evac[rs].append(ev)
                        psd_free[rs] = ev

                for rs in range(4):
                    rt = h * 4 + rs
                    par = rs % 2
                    ffn_t = otile[:, rs * 2048:(rs + 1) * 2048]
                    x2l = a5[:, par * 2048:(par + 1) * 2048]
                    rl = DMA("y", "dr%d" % par, x2l, y[rt * 128:(rt + 1) * 128, :],
                             waits=[TOK("dst"), x2l_free[par], bar_4a[0]])
                    tn4 = ln_stats(rs % 2, ffn_t, ffn_evac[rs])
                    E("v", lambda e, f_=ffn_t, c=rs % 2: e.tensor_scalar(
                        f_, f_, stats[:, 48 + c:49 + c], stats[:, 32 + c:33 + c],
                        ALU.add, ALU.mult), waits=[tn4])
                    fin = E("v", lambda e, f_=ffn_t, x_=x2l: e.tensor_add(x_, x_, f_),
                            waits=[rl], inc="v")
                    st = DMA("y", "df%d" % par, y[rt * 128:(rt + 1) * 128, :], x2l,
                             waits=[fin])
                    x2l_free[par] = st
                    if rs >= 2:
                        h_slotC = st
                wd_last_prev = {0: psd_free[3], 1: psd_free[3]}
                psd_prev = [psd_free[i] for i in range(4)]


        g_sems = ["dx", "dcb", "dg0", "dg1", "dg2", "dd0", "dd1", "dsd"]
        y_sems = ["dmisc", "dstg", "dw0", "dw1", "dcc", "dst", "dr0", "dr1",
                  "df0", "df1"]
        E("g", lambda e: e.nop(), waits=[TOK(sm) for sm in g_sems])
        E("y", lambda e: e.nop(), waits=[TOK(sm) for sm in y_sems])

        def runner(name, eng):
            seen = {}
            for fn, ws, inc, amt in plan.steps[name]:
                for s2, v2 in ws:
                    if seen.get(s2, 0) < v2:
                        eng.wait_ge(sems[s2], v2)
                        seen[s2] = v2
                ins = fn(eng)
                if inc is not None:
                    ins.then_inc(sems[inc], amt)

        @block.gpsimd
        def _(g_):
            runner("g", g_)

        @block.sync
        def _(y_):
            runner("y", y_)

        @block.tensor
        def _(t_):
            runner("t", t_)

        @block.vector
        def _(v_):
            runner("v", v_)

        @block.scalar
        def _(s_):
            runner("s", s_)

    return nc


_NC = None


def kernel(**inputs):
    global _NC
    if _NC is None:
        _NC = build_program()
    nc = _NC

    x = np.asarray(inputs["L0_attn_x"])
    base = {
        "wq": np.asarray(inputs["L0_attn_Wq"]),
        "wk": np.asarray(inputs["L0_attn_Wk"]),
        "wv": np.asarray(inputs["L0_attn_Wv"]),
        "wo": np.asarray(inputs["L0_attn_Wo"]),
        "bq": np.ascontiguousarray(np.asarray(inputs["L0_attn_bq"]).reshape(16, 128)),
        "bk": np.ascontiguousarray(np.asarray(inputs["L0_attn_bk"]).reshape(16, 128)),
        "bv": np.ascontiguousarray(np.asarray(inputs["L0_attn_bv"]).reshape(16, 128)),
        "bo": np.ascontiguousarray(np.asarray(inputs["L0_attn_bo"]).reshape(1, D)),
        "wg": np.asarray(inputs["L0_ffn_w_gate"]),
        "wu": np.asarray(inputs["L0_ffn_w_up"]),
        "wd": np.asarray(inputs["L0_ffn_w_down"]),
        "sg": np.ascontiguousarray(np.asarray(inputs["L0_ffn_s_gate"]).reshape(64, 128)),
        "su": np.ascontiguousarray(np.asarray(inputs["L0_ffn_s_up"]).reshape(64, 128)),
        "sd": np.ascontiguousarray(np.asarray(inputs["L0_ffn_s_down"]).reshape(1, D)),
    }
    in_maps = []
    for c in range(8):
        b, h = c // 2, c % 2
        m = dict(base)
        m["x"] = np.ascontiguousarray(x[b, h * R:(h + 1) * R, :])
        m["msk"] = np.full((128, 1), float(h), dtype=np.float32)
        in_maps.append(m)

    res = run_bass_kernel_spmd(nc, in_maps, core_ids=list(range(8)))
    out = np.empty((4, 2048, D), dtype=np.float32)
    for c in range(8):
        b, h = c // 2, c % 2
        out[b, h * R:(h + 1) * R, :] = res.results[c]["y"]
    return out


# version 10
# speedup vs baseline: 1.0658x; 1.0256x over previous
import numpy as np
import concourse.bass as bass
import concourse.mybir as mybir
from concourse.bass_utils import run_bass_kernel_spmd

F32 = mybir.dt.float32
BF16 = mybir.dt.bfloat16
I8 = mybir.dt.int8
ALU = mybir.AluOpType
AF = mybir.ActivationFunctionType
AXX = mybir.AxisListType.X

D = 2048
INTER = 8192
R = 1024
EPS = 1e-5
SCALE = 128.0 ** -0.5

ENGS = ("g", "y", "t", "v", "s")


class Plan:

    def __init__(self):
        self.steps = {e: [] for e in ENGS}
        self.cnt = {}

    def tok(self, sem):
        return (sem, self.cnt.get(sem, 0))

    def emit(self, eng, fn, waits=(), inc=None, amt=1):
        ws = [w for w in waits if w is not None and w[1] > 0]
        for s, v in ws:
            assert v <= self.cnt.get(s, 0), f"wait on future {s}={v} (now {self.cnt.get(s,0)})"
        tok = None
        if inc is not None:
            self.cnt[inc] = self.cnt.get(inc, 0) + amt
            tok = (inc, self.cnt[inc])
        self.steps[eng].append((fn, ws, inc, amt))
        return tok

    def dma(self, eng, sem, out, in_, waits=(), **kw):
        return self.emit(eng, lambda e: e.dma_start(out, in_, **kw), waits, sem, 16)


def build_program(dbg=False, stop=None):
    nc = bass.Bass("TRN2", target_bir_lowering=False, detect_race_conditions=False)

    xin = nc.dram_tensor("x", [R, D], F32, kind="ExternalInput")
    wq = nc.dram_tensor("wq", [D, D], F32, kind="ExternalInput")
    wk = nc.dram_tensor("wk", [D, D], F32, kind="ExternalInput")
    wv = nc.dram_tensor("wv", [D, D], F32, kind="ExternalInput")
    wo = nc.dram_tensor("wo", [D, D], F32, kind="ExternalInput")
    bq = nc.dram_tensor("bq", [16, 128], F32, kind="ExternalInput")
    bk = nc.dram_tensor("bk", [16, 128], F32, kind="ExternalInput")
    bv = nc.dram_tensor("bv", [16, 128], F32, kind="ExternalInput")
    bo = nc.dram_tensor("bo", [1, D], F32, kind="ExternalInput")
    wg = nc.dram_tensor("wg", [D, INTER], I8, kind="ExternalInput")
    wu = nc.dram_tensor("wu", [D, INTER], I8, kind="ExternalInput")
    wd = nc.dram_tensor("wd", [INTER, D], I8, kind="ExternalInput")
    sg = nc.dram_tensor("sg", [64, 128], F32, kind="ExternalInput")
    su = nc.dram_tensor("su", [64, 128], F32, kind="ExternalInput")
    sd = nc.dram_tensor("sd", [1, D], F32, kind="ExternalInput")
    msk_in = nc.dram_tensor("msk", [128, 1], F32, kind="ExternalInput")
    y = nc.dram_tensor("y", [R, D], F32, kind="ExternalOutput")

    w_bf = {
        "q": nc.dram_tensor("wq_bf", [D, D], BF16),
        "k": nc.dram_tensor("wk_bf", [D, D], BF16),
        "v": nc.dram_tensor("wv_bf", [D, D], BF16),
        "o": nc.dram_tensor("wo_bf", [D, D], BF16),
    }
    bo_bf_d = nc.dram_tensor("bo_bf_d", [1, D], BF16)
    if dbg:
        dbg_xnt = nc.dram_tensor("dbg_xnt", [128, 16 * R], BF16, kind="ExternalOutput")
        dbg_pkv = nc.dram_tensor("dbg_pkv", [128, 16 * R], BF16, kind="ExternalOutput")
        dbg_tot = nc.dram_tensor("dbg_tot", [128, 16], F32, kind="ExternalOutput")
        dbg_ini = nc.dram_tensor("dbg_ini", [128, 16], F32, kind="ExternalOutput")
        dbg_ctx = nc.dram_tensor("dbg_ctx", [128, 16 * R], BF16, kind="ExternalOutput")
        dbg_xn2 = nc.dram_tensor("dbg_xn2", [128, 16 * R], BF16, kind="ExternalOutput")
        dbg_x2 = nc.dram_tensor("dbg_x2", [R, D], F32, kind="ExternalOutput")
        dbg_att = nc.dram_tensor("dbg_att", [128, 4096], F32, kind="ExternalOutput")
    cc_in = nc.dram_tensor("cc_in", [128, 16], F32)
    cc_out = nc.dram_tensor("cc_out", [256, 16], F32)

    plan = Plan()
    E = plan.emit
    DMA = plan.dma
    TOK = plan.tok

    from contextlib import ExitStack
    with ExitStack() as stack:
        en = stack.enter_context
        resid = en(nc.sbuf_tensor("resid", [128, 8 * D], F32))
        xnt = en(nc.sbuf_tensor("xnt", [128, 16 * R], BF16))
        pkv = en(nc.sbuf_tensor("pkv", [128, 16 * R], BF16))
        wtb = en(nc.sbuf_tensor("wtb", [128, 16384], BF16))
        a5 = en(nc.sbuf_tensor("a5", [128, 4096], F32))
        scr = en(nc.sbuf_tensor("scr", [128, 2048], F32))
        identb = en(nc.sbuf_tensor("identb", [128, 128], BF16))
        identf = en(nc.sbuf_tensor("identf", [128, 128], F32))
        onesb = en(nc.sbuf_tensor("onesb", [1, 128], BF16))
        bcol = en(nc.sbuf_tensor("bcol", [128, 48], F32))
        bo_hi = en(nc.sbuf_tensor("bo_hi", [1, D], BF16))
        sgcol = en(nc.sbuf_tensor("sgcol", [128, 64], F32))
        sucol = en(nc.sbuf_tensor("sucol", [128, 64], F32))
        msk = en(nc.sbuf_tensor("msk_sb", [128, 1], F32))
        totals = en(nc.sbuf_tensor("totals", [128, 16], F32))
        initc = en(nc.sbuf_tensor("initc", [128, 16], F32))
        stats = en(nc.sbuf_tensor("stats", [128, 64], F32))
        stage1 = en(nc.sbuf_tensor("stage1", [64, 128], F32))
        dly_v = en(nc.sbuf_tensor("dly_v", [128, 560], F32))
        wnat = en(nc.sbuf_tensor("wnat", [128, 4096], BF16))
        dly_s = en(nc.sbuf_tensor("dly_s", [128, 512], F32))
        psA = en(nc.psum_tensor("psA", [128, 1024], F32))
        psB = en(nc.psum_tensor("psB", [128, 1024], F32))
        psS = en(nc.psum_tensor("psS", [128, 1024], F32))
        psT = en(nc.psum_tensor("psT", [128, 2048], BF16))
        psX = psS
        sem_names = ["t", "v", "s", "g", "cc",
                     "dx", "dck", "dcv", "dcq", "dco", "dcb", "dmisc", "dstg",
                     "dw0", "dw1", "dcc", "dst", "dg0", "dg1", "dg2",
                     "dd0", "dd1", "dsd", "dr0", "dr1", "df0", "df1"]
        sems = {nm: en(nc.semaphore("sem_" + nm)) for nm in sem_names}
        block = en(nc.Block())

        a5b = a5.bitcast(BF16)
        fuse = resid.bitcast(BF16)
        otile = pkv.bitcast(F32)
        scrb = scr.bitcast(BF16)
        wtbf = wtb.bitcast(F32)
        assert list(a5b.shape) == [128, 8192], a5b.shape

        tg1 = E("g", lambda e: e.memset(identb[:], 0.0), inc="g")
        tg2 = E("g", lambda e: e.memset(identf[:], 0.0), inc="g")
        tg3 = E("g", lambda e: e.affine_select(out=identb[:], in_=identb[:],
                                               compare_op=ALU.not_equal, fill=1.0, base=0,
                                               pattern=[[-1, 128]], channel_multiplier=1),
                waits=[tg1], inc="g")
        tg4 = E("g", lambda e: e.affine_select(out=identf[:], in_=identf[:],
                                               compare_op=ALU.not_equal, fill=1.0, base=0,
                                               pattern=[[-1, 128]], channel_multiplier=1),
                waits=[tg2], inc="g")
        t_ident = E("g", lambda e: e.memset(onesb[:], 1.0), waits=[tg3, tg4], inc="g")
        E("v", lambda e: e.memset(dly_v[:], 0.0))
        E("s", lambda e: e.memzero(dly_s[:]))
        E("v", lambda e: e.memset(stats[:, 58:59], 1.0 / D))
        E("v", lambda e: e.memset(stats[:, 59:60], -1.0))
        E("v", lambda e: e.memset(stats[:, 60:61], EPS))
        E("v", lambda e: e.memset(stats[:, 61:62], -0.5))
        t_consts = E("v", lambda e: e.tensor_add(dly_v[:], dly_v[:], dly_v[:]), inc="v")

        for rt in range(8):
            DMA("g", "dx", resid[:, rt * D:(rt + 1) * D],
                xin[rt * 128:(rt + 1) * 128, :])
        t_x = TOK("dx")
        wsrc_of = {"k": wk, "v": wv, "q": wq, "o": wo}
        t_bo_cast = DMA("g", "dcb", bo_bf_d[:, :], bo[:, :])

        DMA("y", "dmisc", msk[:], msk_in[:, :])
        t_msk = TOK("dmisc")

        prep_done = []
        stage_free = None
        for src, n, dst in ((bq, 16, bcol[:, 0:16]), (bk, 16, bcol[:, 16:32]),
                            (bv, 16, bcol[:, 32:48]), (sg, 64, sgcol[:, 0:64]),
                            (su, 64, sucol[:, 0:64])):
            tl = DMA("y", "dstg", stage1[0:n, :], src[0:n, :], waits=[stage_free])
            tt = E("t", lambda e, n_=n: e.transpose(psX[0:128, 0:n_], stage1[0:n_, :],
                                                    identf[0:n_, 0:n_]),
                   waits=[tl, t_ident, TOK("s")], inc="t")
            E("s", lambda e, n_=n, d_=dst: e.copy(d_, psX[:, 0:n_]), waits=[tt])
            ts_ = E("s", lambda e: e.copy(dly_s[:], dly_s[:]), inc="s")
            prep_done.append(ts_)
            stage_free = ts_
        prep_q, prep_k, prep_v, prep_sg, prep_su = prep_done

        def stats_chain(c, width, wv_tok):
            cw = slice(0, 0)
            def a_(out_c, in_c, func, bias=None, scale=None, w=width):
                def fn(e, oc=out_c, ic=in_c, f=func, b=bias, sc=scale, w_=w):
                    kw = {}
                    if b is not None:
                        kw["bias"] = stats[:, b:b + 1] if isinstance(b, int) else b
                    if sc is not None:
                        kw["scale"] = stats[:, sc:sc + 1] if isinstance(sc, int) else sc
                    return e.activation(stats[:, oc:oc + w_], stats[:, ic:ic + w_], f, **kw)
                return fn
            E("s", a_(24 + c, c, AF.Identity, scale=stats[:, 58:59]),
              waits=[wv_tok, t_consts])
            E("s", a_(16 + c, 8 + c, AF.Identity, scale=stats[:, 58:59]))
            E("s", a_(8 + c, 24 + c, AF.Square))
            E("s", a_(8 + c, 8 + c, AF.Identity, scale=stats[:, 59:60]))
            for j in range(width):
                E("s", a_(32 + c + j, 16 + c + j, AF.Identity, bias=8 + c + j, w=1))
            E("s", a_(32 + c, 32 + c, AF.Ln, bias=stats[:, 60:61]))
            E("s", a_(32 + c, 32 + c, AF.Exp, scale=stats[:, 61:62]))
            E("s", a_(48 + c, 24 + c, AF.Identity, scale=stats[:, 59:60]))
            for j in range(width):
                E("s", a_(40 + c + j, 48 + c + j, AF.Identity, scale=32 + c + j, w=1))
            return E("s", lambda e: e.copy(dly_s[:], dly_s[:]), inc="s")

        scr_free = [("v", 0)]

        def ln_sums(c, src_ap, first_waits):
            E("v", lambda e: e.reduce_sum(stats[:, c:c + 1], src_ap, axis=AXX),
              waits=list(first_waits))
            E("s", lambda e: e.activation(scr[:, 0:1024], src_ap[:, 0:1024], AF.Square),
              waits=list(first_waits) + [scr_free[0]])
            sqb = E("s", lambda e: e.activation(scr[:, 1024:2048], src_ap[:, 1024:2048],
                                                AF.Square), inc="s")
            E("v", lambda e: e.reduce_sum(stats[:, 8 + c:9 + c], scr[:], axis=AXX),
              waits=[sqb])
            tok = E("v", lambda e: e.tensor_add(dly_v[:], dly_v[:], dly_v[:]), inc="v")
            scr_free[0] = tok
            return tok

        def ln_stats(c, src_ap, first_waits):
            return stats_chain(c, 1, ln_sums(c, src_ap, first_waits))

        def transpose_16(src_bf, rt, first_wait, psT_free):
            fw = list(first_wait) if isinstance(first_wait, list) else [first_wait]
            last_ev = None
            for b4 in range(4):
                sl = b4 % 2
                tt = None
                for j in range(4):
                    m = b4 * 4 + j
                    tt = E("t", lambda e, i_=src_bf[:, m * 128:(m + 1) * 128],
                           o_=psT[:, sl * 1024 + j * 128:sl * 1024 + (j + 1) * 128]:
                           e.transpose(o_, i_, identb[:]),
                           waits=fw + [psT_free[sl], t_ident],
                           inc="t" if j == 3 else None)
                ev = E("v", lambda e, m0=b4 * 4, r0=rt * 128, sl_=sl: e.tensor_copy(
                    bass.AP(xnt, m0 * R + r0, [[16 * R, 128], [R, 4], [1, 128]]),
                    bass.AP(psT, sl_ * 1024, [[2048, 128], [128, 4], [1, 128]])),
                    waits=[tt], inc="v")
                psT_free[sl] = ev
                last_ev = ev
            return last_ev

        for rt in range(4):
            ln_sums(rt, resid[:, rt * D:(rt + 1) * D], [t_x])
        t_stats1a = stats_chain(0, 4, TOK("v"))
        for rt in range(4, 8):
            ln_sums(rt, resid[:, rt * D:(rt + 1) * D], [t_x])
        t_stats1b = stats_chain(4, 4, TOK("v"))

        psT_free = [TOK("v"), TOK("v")]
        xn_free = [TOK("t"), TOK("t")]
        xnt_rh = [None, None]
        for rt in range(8):
            par = rt % 2
            xnb = a5b[:, par * 2048:(par + 1) * 2048]
            tn = E("s", lambda e, rt_=rt, o_=xnb: e.activation(
                o_, resid[:, rt_ * D:(rt_ + 1) * D], AF.Identity,
                bias=stats[:, 40 + rt_:41 + rt_], scale=stats[:, 32 + rt_:33 + rt_]),
                waits=[t_stats1a if rt < 4 else t_stats1b, xn_free[par]], inc="s")
            ev = transpose_16(xnb, rt, tn, psT_free)
            xn_free[par] = TOK("t")
            if rt == 3:
                xnt_rh[0] = ev
            if rt == 7:
                xnt_rh[1] = ev
        bar_p1 = [TOK("t"), TOK("v"), TOK("s")]
        if dbg:
            bar_p1.append(DMA("y", "dmisc", dbg_xnt[:, :], xnt[:], waits=bar_p1))

        _lvl = {"p1": 1, "p2": 2, "p3": 3, None: 9}[stop]
        if dbg and _lvl == 1:
            for rt_ in range(8):
                DMA("y", "dmisc", dbg_x2[rt_ * 128:(rt_ + 1) * 128, :],
                    resid[:, rt_ * D:(rt_ + 1) * D], waits=bar_p1)
            DMA("y", "dmisc", dbg_tot[:, :], stats[:, 0:16], waits=bar_p1)
            DMA("y", "dmisc", dbg_ini[:, :], stats[:, 24:40], waits=bar_p1)
            DMA("y", "dmisc", dbg_ctx[:, 0:32], stats[:, 48:64].bitcast(BF16), waits=bar_p1)
            DMA("y", "dmisc", dbg_pkv[:, 0:4096], a5b[:, 0:4096], waits=bar_p1)
        if _lvl >= 2:
            wslot_free = [("t", 0), ("t", 0)]
            wnat_free = [("t", 0), ("t", 0)]

            def load_wt_group_pe(key, g, p):
                wsrc = wsrc_of[key]
                ev = None
                for ib in range(4):
                    eb = g * 4 + ib
                    ns = eb % 2
                    ld = DMA("g", "dg%d" % ns,
                             wnat[:, ns * 2048:(ns + 1) * 2048],
                             wsrc[eb * 128:(eb + 1) * 128, :],
                             waits=[wnat_free[ns]])
                    for tb in range(4):
                        psl = tb % 2
                        tt = None
                        for j in range(4):
                            t_ = tb * 4 + j
                            tt = E("t", lambda e, ns_=ns, t2=t_, psl_=psl, j_=j:
                                   e.transpose(
                                       psT[:, psl_ * 1024 + j_ * 128:
                                           psl_ * 1024 + (j_ + 1) * 128],
                                       wnat[:, ns_ * 2048 + t2 * 128:
                                            ns_ * 2048 + (t2 + 1) * 128],
                                       identb[:]),
                                   waits=[ld, psT_free[psl], t_ident],
                                   inc="t" if j == 3 else None)
                        ev = E("v", lambda e, p_=p, tb_=tb, ib_=ib, psl_=psl:
                               e.tensor_copy(
                                   bass.AP(wtb, p_ * 8192 + tb_ * 4 * 512 + ib_ * 128,
                                           [[16384, 128], [512, 4], [1, 128]]),
                                   bass.AP(psT, psl_ * 1024,
                                           [[2048, 128], [128, 4], [1, 128]])),
                               waits=[tt, wslot_free[p] if (ib == 0 and tb == 0) else None],
                               inc="v")
                        psT_free[psl] = ev
                    wnat_free[ns] = tt
                return ev

            def load_wt_group(wbf_dram, g, extra_wait):
                p = g % 2
                first = True
                tok = None
                for d in range(16):
                    tok = DMA("y", "dw%d" % p,
                              wtb[:, p * 8192 + d * 512:p * 8192 + d * 512 + 512],
                              wbf_dram[g * 512:(g + 1) * 512, d * 128:(d + 1) * 128],
                              waits=[wslot_free[p], extra_wait] if first else (),
                              transpose=True)
                    first = False
                return tok

            def proj_mms(m, ps, extra_waits):
                p = (m // 4) % 2
                last = None
                for rh in range(2):
                    for d in range(16):
                        if d == 0:
                            ws = ([xnt_rh[rh]] + (list(extra_waits) if rh == 0 else []))
                        else:
                            ws = ()
                        last = E("t", lambda e, d_=d, rh_=rh, m_=m, p_=p, ps_=ps: e.matmul(
                            ps_[:, rh_ * 512:(rh_ + 1) * 512],
                            wtb[:, p_ * 8192 + d_ * 512 + (m_ % 4) * 128:
                                p_ * 8192 + d_ * 512 + (m_ % 4) * 128 + 128],
                            xnt[:, d_ * R + rh_ * 512:d_ * R + rh_ * 512 + 512],
                            start=(d_ == 0), stop=(d_ == 15)),
                            waits=ws,
                            inc="t" if (rh == 1 and d == 15) else None)
                return last

            ps_free = {0: ("v", 0), 1: ("v", 0), 2: ("v", 0)}
            PSROT = [psA, psB, psS]
            TEXP = [a5b[:, 0:1024], a5b[:, 1024:2048]]
            TRELU = a5b[:, 2048:3072]
            TV = [a5b[:, 3072:4096], a5b[:, 4096:5120]]
            PHIQ = a5b[:, 5120:6144]
            texp_free = [TOK("v"), TOK("v")]
            tv_free = [TOK("v"), TOK("v")]

            def wt_pass(key, evac, first_extra, preloaded=None, nrot=3):
                loaded = preloaded if preloaded is not None else {
                    0: load_wt_group_pe(key, 0, 0),
                    1: load_wt_group_pe(key, 1, 1)}
                for m in range(16):
                    g = m // 4
                    sl = m % nrot
                    ps = PSROT[sl]
                    extra = [loaded[g], ps_free[sl]] + (first_extra if m == 0 else [])
                    tmm = proj_mms(m, ps, extra)
                    if m % 4 == 3:
                        wslot_free[g % 2] = tmm
                        if g + 2 < 4:
                            loaded[g + 2] = load_wt_group_pe(key, g + 2, g % 2)
                    evac(m, ps, tmm, sl)

            def evac_k(m, ps, tmm, sl):
                par = m % 2
                te = E("s", lambda e, ps_=ps, m_=m, p_=par: e.activation(
                    TEXP[p_], ps_[:], AF.Exp, bias=bcol[:, 16 + m_:17 + m_]),
                    waits=[tmm, texp_free[par], prep_k], inc="s")
                tr = E("v", lambda e, ps_=ps, m_=m: e.tensor_scalar(
                    TRELU, ps_[:], bcol[:, 16 + m_:17 + m_], 0.0, ALU.add, ALU.max),
                    waits=[te], inc="v")
                ps_free[sl] = tr
                ph = E("v", lambda e, m_=m, p_=par: e.scalar_tensor_tensor(
                    pkv[:, m_ * R:(m_ + 1) * R], TEXP[p_], 1.0, TRELU, ALU.min, ALU.add),
                    inc="v")
                texp_free[par] = ph

            wt_pass("k", evac_k, [])

            tot_toks = []

            def evac_v(m, ps, tmm, sl):
                par = m % 2
                tv_ = E("s", lambda e, ps_=ps, m_=m, p_=par: e.activation(
                    TV[p_], ps_[:], AF.Identity, bias=bcol[:, 32 + m_:33 + m_]),
                    waits=[tmm, tv_free[par], prep_v], inc="s")
                ps_free[sl] = tv_
                mu = E("v", lambda e, m_=m, p_=par: e.tensor_mul(
                    pkv[:, m_ * R:(m_ + 1) * R], pkv[:, m_ * R:(m_ + 1) * R], TV[p_]),
                    waits=[tv_], inc="v")
                tv_free[par] = mu
                E("v", lambda e, m_=m: e.reduce_sum(
                    totals[:, m_:m_ + 1], pkv[:, m_ * R:(m_ + 1) * R], axis=AXX))
                if m == 15:
                    tot_toks.append(E("v", lambda e: e.tensor_add(dly_v[:], dly_v[:], dly_v[:]),
                                     inc="v"))
                else:
                    tot_toks.append(TOK("v"))

            wt_pass("v", evac_v, [])

            psS_clear = ps_free[2]
            q_preload = {0: load_wt_group_pe("q", 0, 0),
                         1: load_wt_group_pe("q", 1, 1)}

            if dbg:
                dtok = DMA("y", "dmisc", dbg_pkv[:, :], pkv[:], waits=[tot_toks[-1]])
                DMA("y", "dmisc", dbg_tot[:, :], totals[:], waits=[tot_toks[-1]])

            d1 = DMA("y", "dcc", cc_in[:, :], totals[:, 0:16], waits=[tot_toks[-1]])
            ccd = E("g", lambda e: e.collective_compute(
                "AllGather", ALU.bypass, replica_groups=[[0, 1], [2, 3], [4, 5], [6, 7]],
                ins=[cc_in.ap().opt()], outs=[cc_out.ap().opt()]),
                waits=[d1], inc="cc")
            d2 = DMA("y", "dcc", initc[:, 0:16], cc_out[0:128, :], waits=[ccd])
            E("s", lambda e: e.activation(initc[:, 0:16], initc[:, 0:16], AF.Identity,
                                          scale=msk[:]),
              waits=[d2, t_msk])
            t_init = E("s", lambda e: e.copy(dly_s[:], dly_s[:]), inc="s")
            t_bo = DMA("y", "dmisc", bo_hi[:], bo_bf_d[:, :], waits=[t_bo_cast])
            if dbg:
                DMA("y", "dmisc", dbg_ini[:, :], initc[:], waits=[t_init])

            ctx_last = [None]

            def evac_q(m, ps, tmm, sl):
                par = m % 2
                te = E("s", lambda e, ps_=ps, m_=m, p_=par: e.activation(
                    TEXP[p_], ps_[:], AF.Exp, bias=bcol[:, m_:1 + m_]),
                    waits=[tmm, texp_free[par], prep_q], inc="s")
                tr = E("v", lambda e, ps_=ps, m_=m: e.tensor_scalar(
                    TRELU, ps_[:], bcol[:, m_:1 + m_], 0.0, ALU.add, ALU.max),
                    waits=[te], inc="v")
                ps_free[sl] = tr
                E("v", lambda e, p_=par: e.scalar_tensor_tensor(
                    PHIQ, TEXP[p_], 1.0, TRELU, ALU.min, ALU.add))
                E("v", lambda e, m_=m: e.tensor_tensor_scan(
                    psS[:], pkv[:, m_ * R:(m_ + 1) * R], pkv[:, m_ * R:(m_ + 1) * R],
                    initc[:, m_:m_ + 1], ALU.add, ALU.bypass),
                    waits=[t_init, psS_clear] + ([dtok] if dbg else []))
                cx = E("v", lambda e, m_=m: e.scalar_tensor_tensor(
                    pkv[:, m_ * R:(m_ + 1) * R], psS[:], SCALE, PHIQ, ALU.mult, ALU.mult),
                    inc="v")
                texp_free[par] = cx
                ctx_last[0] = cx

            wt_pass("q", evac_q, [psS_clear], preloaded=q_preload, nrot=2)
            bar_ctx = ctx_last[0]


        if _lvl >= 3:
            PS3 = [psA[:, 0:512], psA[:, 512:1024], psB[:, 0:512], psB[:, 512:1024]]
            ps3_free = {i: TOK("v") for i in range(4)}
            x2_store = []
            tnorm_toks = []
            attn_bf = xnt

            wo_loaded = {0: load_wt_group_pe("o", 0, 0),
                         1: load_wt_group_pe("o", 1, 1)}
            evac_toks = {rt: [] for rt in range(8)}
            for g in range(4):
                p = g % 2
                last = None
                for rt in range(8):
                    slot = rt % 4
                    for d in range(16):
                        E("t", lambda e, d_=d, rt_=rt, p_=p, sl=slot: e.matmul(
                            PS3[sl],
                            pkv[:, d_ * R + rt_ * 128:d_ * R + rt_ * 128 + 128],
                            wtb[:, p_ * 8192 + d_ * 512:p_ * 8192 + d_ * 512 + 512],
                            start=(d_ == 0), stop=False),
                            waits=[wo_loaded[g], ps3_free[slot], bar_ctx, t_bo]
                            if d == 0 else ())
                    last = E("t", lambda e, g_=g, sl=slot: e.matmul(
                        PS3[sl], onesb[0:1, :], bo_hi[0:1, g_ * 512:(g_ + 1) * 512],
                        start=False, stop=True), inc="t")
                    ev = E("s", lambda e, g_=g, rt_=rt, sl=slot: e.copy(
                        attn_bf[:, rt_ * 2048 + g_ * 512:rt_ * 2048 + g_ * 512 + 512],
                        PS3[sl]), waits=[last], inc="s")
                    ps3_free[slot] = ev
                    evac_toks[rt].append(ev)
                wslot_free[p] = last
                if g + 2 < 4:
                    wo_loaded[g + 2] = load_wt_group_pe("o", g + 2, g % 2)

            if dbg:
                datt = DMA("y", "dmisc", dbg_att[:, :],
                           attn_bf[:, 0:4096], waits=evac_toks[0] + evac_toks[1])

            for rt in range(8):
                c = rt % 2
                at = attn_bf[:, rt * 2048:(rt + 1) * 2048]
                tn2 = ln_stats(c, at, evac_toks[rt] +
                               ([datt] if (dbg and rt < 2) else []))
                E("v", lambda e, a_=at, c_=c: e.tensor_scalar(
                    scr[:], a_, stats[:, 48 + c_:49 + c_], stats[:, 32 + c_:33 + c_],
                    ALU.add, ALU.mult), waits=[tn2])
                add_t = E("v", lambda e, rt_=rt: e.tensor_add(
                    resid[:, rt_ * D:(rt_ + 1) * D], resid[:, rt_ * D:(rt_ + 1) * D],
                    scr[:]), inc="v")
                scr_free[0] = add_t
                x2_store.append(DMA("y", "dst", y[rt * 128:(rt + 1) * 128, :],
                                    resid[:, rt * D:(rt + 1) * D], waits=[add_t]))
                tn3 = ln_stats(2 + c, resid[:, rt * D:(rt + 1) * D], [add_t])
                xn2b = fuse[:, rt * 4096:rt * 4096 + 2048]
                tnorm = E("s", lambda e, rt_=rt, o_=xn2b, c_=2 + c: e.activation(
                    o_, resid[:, rt_ * D:(rt_ + 1) * D], AF.Identity,
                    bias=stats[:, 40 + c_:41 + c_], scale=stats[:, 32 + c_:33 + c_]),
                    waits=[tn3, x2_store[-1]], inc="s")
                tnorm_toks.append(tnorm)

            t_applies = TOK("v")
            xn2T_done = []
            xnt2_rh = [None, None]
            for rt in range(8):
                xn2T_done.append(transpose_16(fuse[:, rt * 4096:rt * 4096 + 2048], rt,
                                              [tnorm_toks[rt], t_applies], psT_free))
            xnt2_rh[0] = xn2T_done[3]
            xnt2_rh[1] = xn2T_done[7]

            bar_p3 = [TOK("t"), TOK("v"), TOK("s"), x2_store[-1]]
            if dbg:
                bar_p3.append(DMA("y", "dmisc", dbg_ctx[:, :], pkv[:], waits=[bar_ctx]))
                bar_p3.append(DMA("y", "dmisc", dbg_xn2[:, :], xnt[:], waits=bar_p3[:4]))
                for rt_ in range(8):
                    tok_ = DMA("y", "dmisc", dbg_x2[rt_ * 128:(rt_ + 1) * 128, :],
                               resid[:, rt_ * D:(rt_ + 1) * D], waits=bar_p3[:4])
                bar_p3.append(tok_)


        if _lvl >= 9:
            GU = [(wtb, 0), (wtb, 8192), (a5b, 0)]
            TG = [scrb[:, 0:512], scrb[:, 512:1024]]
            TU = [scrb[:, 1024:1536], scrb[:, 1536:2048]]

            def load_gu(wdram, s_, slot, wait_tok):
                parent, base = GU[slot]
                ncols = parent.shape[1]
                first = True
                tok = None
                for t2 in range(2):
                    tok = DMA("g", "dg%d" % slot,
                              bass.AP(parent, base + t2 * 8 * 512,
                                      [[ncols, 128], [512, 8], [1, 512]]),
                              bass.AP(wdram, t2 * 8 * 128 * INTER + s_ * 512,
                                      [[INTER, 128], [128 * INTER, 8], [1, 512]]),
                              waits=[wait_tok] if first else ())
                    first = False
                return tok

            def gu_slice(slot, d, icw):
                parent, base = GU[slot]
                return parent[:, base + d * 512 + icw * 128: base + d * 512 + icw * 128 + 128]

            bar_h = bar_p3
            h_slotC = bar_p3[0]
    
            x2l_free = [None, None]
            wd_last_prev = {0: bar_p3[0], 1: bar_p3[0]}
            psd_prev = None
            for h in range(2):
                if h == 0:
                    slot_last = {0: wslot_free[0], 1: wslot_free[1], 2: bar_ctx}
                else:
                    slot_last = {0: wd_last_prev[0], 1: wd_last_prev[1], 2: h_slotC}
                loaded = {(0, 0): load_gu(wg, 0, 0, slot_last[0]),
                          (0, 1): load_gu(wu, 0, 1, slot_last[1])}
                slot_of = {(0, 0): 0, (0, 1): 1}
                for s_ in range(1, 16):
                    slot_of[(s_, 0)] = (2 * s_) % 3
                    slot_of[(s_, 1)] = (2 * s_ + 1) % 3
                tg_free = [TOK("v"), TOK("v")]
                tu_free = [TOK("v"), TOK("v")]
                if psd_prev is None:
                    psg_free = [TOK("s"), TOK("s")]
                    psu_free = [TOK("v"), TOK("v")]
                else:
                    psg_free = [psd_prev[0], psd_prev[2]]
                    psu_free = [psd_prev[1], psd_prev[3]]
                fuse_last = [None]
                for s_ in range(16):
                    slg, slu = slot_of[(s_, 0)], slot_of[(s_, 1)]
                    lastg = lastu = None
                    for icw in range(4):
                        ic = s_ * 4 + icw
                        par = ic % 2
                        psg = (psA if par == 0 else psB)[:, 0:512]
                        psu = (psA if par == 0 else psB)[:, 512:1024]
                        for d in range(16):
                            lastg = E("t", lambda e, d_=d, sl=slg, ic_=icw, ps_=psg, h_=h: e.matmul(
                                ps_, gu_slice(sl, d_, ic_),
                                xnt[:, d_ * R + h_ * 512:d_ * R + h_ * 512 + 512],
                                start=(d_ == 0), stop=(d_ == 15)),
                                waits=([loaded[(s_, 0)], psg_free[par]] +
                                       ([xnt2_rh[h]] if (s_ == 0 and icw == 0) else []))
                                if d == 0 else (),
                                inc="t" if d == 15 else None)
                        for d in range(16):
                            lastu = E("t", lambda e, d_=d, sl=slu, ic_=icw, ps_=psu, h_=h: e.matmul(
                                ps_, gu_slice(sl, d_, ic_),
                                xnt[:, d_ * R + h_ * 512:d_ * R + h_ * 512 + 512],
                                start=(d_ == 0), stop=(d_ == 15)),
                                waits=[loaded[(s_, 1)], psu_free[par]] if d == 0 else (),
                                inc="t" if d == 15 else None)
                        tg_t = E("s", lambda e, ps_=psg, ic_=ic, p_=par: e.activation(
                            TG[p_], ps_, AF.Silu, scale=sgcol[:, ic_:ic_ + 1]),
                            waits=[lastg, tg_free[par], prep_sg], inc="s")
                        psg_free[par] = tg_t
                        tu_t = E("v", lambda e, ps_=psu, ic_=ic, p_=par: e.tensor_scalar(
                            TU[p_], ps_, sucol[:, ic_:ic_ + 1], None, ALU.mult),
                            waits=[lastu, tu_free[par], prep_su], inc="v")
                        psu_free[par] = tu_t
                        f_t = E("v", lambda e, ic_=ic, p_=par: e.tensor_mul(
                            fuse[:, ic_ * 512:(ic_ + 1) * 512], TG[p_], TU[p_]),
                            waits=[tg_t], inc="v")
                        tg_free[par] = f_t
                        tu_free[par] = f_t
                        fuse_last[0] = f_t
                    slot_last[slg] = lastg
                    slot_last[slu] = lastu
                    if s_ + 1 < 16:
                        ng, nu = slot_of[(s_ + 1, 0)], slot_of[(s_ + 1, 1)]
                        loaded[(s_ + 1, 0)] = load_gu(wg, s_ + 1, ng, slot_last[ng])
                        loaded[(s_ + 1, 1)] = load_gu(wu, s_ + 1, nu, slot_last[nu])
                bar_4a = [TOK("t"), TOK("v"), TOK("s")]

                sdb = wtbf[:, 2048:4096]
                t_sdb = DMA("g", "dsd", sdb, bass.AP(sd, 0, [[0, 128], [1, 2048]]),
                            waits=[bar_4a[0]])
                WDSL = [(0, None), (2048, None)]

                def load_wd(s_, dgi, wait_tok):
                    p = s_ % 2
                    return DMA("g", "dd%d" % p,
                               bass.AP(wtb, WDSL[p][0], [[16384, 128], [512, 4], [1, 512]]),
                               bass.AP(wd, s_ * 512 * D + dgi * 512,
                                       [[D, 128], [128 * D, 4], [1, 512]]),
                               waits=[wait_tok])

                PSD = [psA[:, 0:512], psA[:, 512:1024], psB[:, 0:512], psB[:, 512:1024]]
                psd_free = {i: bar_4a[1] for i in range(4)}
                wd_last = {0: bar_4a[0], 1: bar_4a[0]}
                ffn_evac = {rs: [] for rs in range(4)}
                for dgi in range(4):
                    wloaded = {0: load_wd(0, dgi, wd_last[0]),
                               1: load_wd(1, dgi, wd_last[1])}
                    group_end = {}
                    for s_ in range(16):
                        p = s_ % 2
                        for icw in range(4):
                            ic = s_ * 4 + icw
                            for rs in range(4):
                                first = (s_ == 0 and icw == 0)
                                lastmm = (s_ == 15 and icw == 3)
                                if first:
                                    ws = [wloaded[p], psd_free[rs]] + \
                                         (bar_4a + [fuse_last[0]] if (rs == 0 and dgi == 0) else [])
                                elif icw == 0 and rs == 0:
                                    ws = [wloaded[p]]
                                else:
                                    ws = ()
                                tok = E("t", lambda e, ic_=ic, rs_=rs, p_=p, ic2=icw,
                                        fl=first, ll=lastmm: e.matmul(
                                    PSD[rs_],
                                    fuse[:, ic_ * 512 + rs_ * 128:ic_ * 512 + rs_ * 128 + 128],
                                    wtb[:, WDSL[p_][0] + ic2 * 512:WDSL[p_][0] + ic2 * 512 + 512],
                                    start=fl, stop=ll),
                                    waits=ws,
                                    inc="t" if (icw == 3 and rs == 3) else None)
                                if icw == 3 and rs == 3:
                                    group_end[s_] = tok
                        wd_last[p] = group_end[s_]
                        if s_ + 2 < 16:
                            wloaded[p] = load_wd(s_ + 2, dgi, wd_last[p])
                    for rs in range(4):
                        ev = E("v", lambda e, rs_=rs, dg_=dgi: e.scalar_tensor_tensor(
                            otile[:, rs_ * 2048 + dg_ * 512:rs_ * 2048 + dg_ * 512 + 512],
                            PSD[rs_], 1.0, sdb[:, dg_ * 512:(dg_ + 1) * 512],
                            ALU.bypass, ALU.mult),
                            waits=[group_end[15], t_sdb], inc="v")
                        ffn_evac[rs].append(ev)
                        psd_free[rs] = ev

                for rs in range(4):
                    rt = h * 4 + rs
                    par = rs % 2
                    ffn_t = otile[:, rs * 2048:(rs + 1) * 2048]
                    x2l = a5[:, par * 2048:(par + 1) * 2048]
                    rl = DMA("y", "dr%d" % par, x2l, y[rt * 128:(rt + 1) * 128, :],
                             waits=[TOK("dst"), x2l_free[par], bar_4a[0]])
                    tn4 = ln_stats(rs % 2, ffn_t, ffn_evac[rs])
                    E("v", lambda e, f_=ffn_t, c=rs % 2: e.tensor_scalar(
                        f_, f_, stats[:, 48 + c:49 + c], stats[:, 32 + c:33 + c],
                        ALU.add, ALU.mult), waits=[tn4])
                    fin = E("v", lambda e, f_=ffn_t, x_=x2l: e.tensor_add(x_, x_, f_),
                            waits=[rl], inc="v")
                    st = DMA("y", "df%d" % par, y[rt * 128:(rt + 1) * 128, :], x2l,
                             waits=[fin])
                    x2l_free[par] = st
                    if rs >= 2:
                        h_slotC = st
                wd_last_prev = {0: psd_free[3], 1: psd_free[3]}
                psd_prev = [psd_free[i] for i in range(4)]


        g_sems = ["dx", "dcb", "dg0", "dg1", "dg2", "dd0", "dd1", "dsd"]
        y_sems = ["dmisc", "dstg", "dw0", "dw1", "dcc", "dst", "dr0", "dr1",
                  "df0", "df1"]
        E("g", lambda e: e.nop(), waits=[TOK(sm) for sm in g_sems])
        E("y", lambda e: e.nop(), waits=[TOK(sm) for sm in y_sems])

        def runner(name, eng):
            seen = {}
            for fn, ws, inc, amt in plan.steps[name]:
                for s2, v2 in ws:
                    if seen.get(s2, 0) < v2:
                        eng.wait_ge(sems[s2], v2)
                        seen[s2] = v2
                ins = fn(eng)
                if inc is not None:
                    ins.then_inc(sems[inc], amt)

        @block.gpsimd
        def _(g_):
            runner("g", g_)

        @block.sync
        def _(y_):
            runner("y", y_)

        @block.tensor
        def _(t_):
            runner("t", t_)

        @block.vector
        def _(v_):
            runner("v", v_)

        @block.scalar
        def _(s_):
            runner("s", s_)

    return nc


_NC = None


def kernel(**inputs):
    global _NC
    if _NC is None:
        _NC = build_program()
    nc = _NC

    x = np.asarray(inputs["L0_attn_x"])
    base = {
        "wq": np.asarray(inputs["L0_attn_Wq"]),
        "wk": np.asarray(inputs["L0_attn_Wk"]),
        "wv": np.asarray(inputs["L0_attn_Wv"]),
        "wo": np.asarray(inputs["L0_attn_Wo"]),
        "bq": np.ascontiguousarray(np.asarray(inputs["L0_attn_bq"]).reshape(16, 128)),
        "bk": np.ascontiguousarray(np.asarray(inputs["L0_attn_bk"]).reshape(16, 128)),
        "bv": np.ascontiguousarray(np.asarray(inputs["L0_attn_bv"]).reshape(16, 128)),
        "bo": np.ascontiguousarray(np.asarray(inputs["L0_attn_bo"]).reshape(1, D)),
        "wg": np.asarray(inputs["L0_ffn_w_gate"]),
        "wu": np.asarray(inputs["L0_ffn_w_up"]),
        "wd": np.asarray(inputs["L0_ffn_w_down"]),
        "sg": np.ascontiguousarray(np.asarray(inputs["L0_ffn_s_gate"]).reshape(64, 128)),
        "su": np.ascontiguousarray(np.asarray(inputs["L0_ffn_s_up"]).reshape(64, 128)),
        "sd": np.ascontiguousarray(np.asarray(inputs["L0_ffn_s_down"]).reshape(1, D)),
    }
    in_maps = []
    for c in range(8):
        b, h = c // 2, c % 2
        m = dict(base)
        m["x"] = np.ascontiguousarray(x[b, h * R:(h + 1) * R, :])
        m["msk"] = np.full((128, 1), float(h), dtype=np.float32)
        in_maps.append(m)

    res = run_bass_kernel_spmd(nc, in_maps, core_ids=list(range(8)))
    out = np.empty((4, 2048, D), dtype=np.float32)
    for c in range(8):
        b, h = c // 2, c % 2
        out[b, h * R:(h + 1) * R, :] = res.results[c]["y"]
    return out


# version 11 (final)
# speedup vs baseline: 1.0793x; 1.0127x over previous
import numpy as np
import concourse.bass as bass
import concourse.mybir as mybir
from concourse.bass_utils import run_bass_kernel_spmd

F32 = mybir.dt.float32
BF16 = mybir.dt.bfloat16
I8 = mybir.dt.int8
ALU = mybir.AluOpType
AF = mybir.ActivationFunctionType
AXX = mybir.AxisListType.X

D = 2048
INTER = 8192
R = 1024
EPS = 1e-5
SCALE = 128.0 ** -0.5

ENGS = ("g", "y", "t", "v", "s")


class Plan:

    def __init__(self):
        self.steps = {e: [] for e in ENGS}
        self.cnt = {}

    def tok(self, sem):
        return (sem, self.cnt.get(sem, 0))

    def emit(self, eng, fn, waits=(), inc=None, amt=1):
        ws = [w for w in waits if w is not None and w[1] > 0]
        for s, v in ws:
            assert v <= self.cnt.get(s, 0), f"wait on future {s}={v} (now {self.cnt.get(s,0)})"
        tok = None
        if inc is not None:
            self.cnt[inc] = self.cnt.get(inc, 0) + amt
            tok = (inc, self.cnt[inc])
        self.steps[eng].append((fn, ws, inc, amt))
        return tok

    def dma(self, eng, sem, out, in_, waits=(), **kw):
        return self.emit(eng, lambda e: e.dma_start(out, in_, **kw), waits, sem, 16)


def build_program(dbg=False, stop=None):
    nc = bass.Bass("TRN2", target_bir_lowering=False, detect_race_conditions=False)

    xin = nc.dram_tensor("x", [R, D], F32, kind="ExternalInput")
    wq = nc.dram_tensor("wq", [D, D], F32, kind="ExternalInput")
    wk = nc.dram_tensor("wk", [D, D], F32, kind="ExternalInput")
    wv = nc.dram_tensor("wv", [D, D], F32, kind="ExternalInput")
    wo = nc.dram_tensor("wo", [D, D], F32, kind="ExternalInput")
    bq = nc.dram_tensor("bq", [16, 128], F32, kind="ExternalInput")
    bk = nc.dram_tensor("bk", [16, 128], F32, kind="ExternalInput")
    bv = nc.dram_tensor("bv", [16, 128], F32, kind="ExternalInput")
    bo = nc.dram_tensor("bo", [1, D], F32, kind="ExternalInput")
    wg = nc.dram_tensor("wg", [D, INTER], I8, kind="ExternalInput")
    wu = nc.dram_tensor("wu", [D, INTER], I8, kind="ExternalInput")
    wd = nc.dram_tensor("wd", [INTER, D], I8, kind="ExternalInput")
    sg = nc.dram_tensor("sg", [64, 128], F32, kind="ExternalInput")
    su = nc.dram_tensor("su", [64, 128], F32, kind="ExternalInput")
    sd = nc.dram_tensor("sd", [1, D], F32, kind="ExternalInput")
    msk_in = nc.dram_tensor("msk", [128, 1], F32, kind="ExternalInput")
    y = nc.dram_tensor("y", [R, D], F32, kind="ExternalOutput")

    w_bf = {
        "q": nc.dram_tensor("wq_bf", [D, D], BF16),
        "k": nc.dram_tensor("wk_bf", [D, D], BF16),
        "v": nc.dram_tensor("wv_bf", [D, D], BF16),
        "o": nc.dram_tensor("wo_bf", [D, D], BF16),
    }
    bo_bf_d = nc.dram_tensor("bo_bf_d", [1, D], BF16)
    if dbg:
        dbg_xnt = nc.dram_tensor("dbg_xnt", [128, 16 * R], BF16, kind="ExternalOutput")
        dbg_pkv = nc.dram_tensor("dbg_pkv", [128, 16 * R], BF16, kind="ExternalOutput")
        dbg_tot = nc.dram_tensor("dbg_tot", [128, 16], F32, kind="ExternalOutput")
        dbg_ini = nc.dram_tensor("dbg_ini", [128, 16], F32, kind="ExternalOutput")
        dbg_ctx = nc.dram_tensor("dbg_ctx", [128, 16 * R], BF16, kind="ExternalOutput")
        dbg_xn2 = nc.dram_tensor("dbg_xn2", [128, 16 * R], BF16, kind="ExternalOutput")
        dbg_x2 = nc.dram_tensor("dbg_x2", [R, D], F32, kind="ExternalOutput")
        dbg_att = nc.dram_tensor("dbg_att", [128, 4096], BF16, kind="ExternalOutput")
    cc_in = nc.dram_tensor("cc_in", [128, 16], F32)
    cc_out = nc.dram_tensor("cc_out", [256, 16], F32)

    plan = Plan()
    E = plan.emit
    DMA = plan.dma
    TOK = plan.tok

    from contextlib import ExitStack
    with ExitStack() as stack:
        en = stack.enter_context
        resid = en(nc.sbuf_tensor("resid", [128, 8 * D], F32))
        xnt = en(nc.sbuf_tensor("xnt", [128, 16 * R], BF16))
        pkv = en(nc.sbuf_tensor("pkv", [128, 16 * R], BF16))
        wtb = en(nc.sbuf_tensor("wtb", [128, 16384], BF16))
        a5 = en(nc.sbuf_tensor("a5", [128, 4096], F32))
        scr = en(nc.sbuf_tensor("scr", [128, 2048], F32))
        identb = en(nc.sbuf_tensor("identb", [128, 128], BF16))
        identf = en(nc.sbuf_tensor("identf", [128, 128], F32))
        onesb = en(nc.sbuf_tensor("onesb", [1, 128], BF16))
        bcol = en(nc.sbuf_tensor("bcol", [128, 48], F32))
        bo_hi = en(nc.sbuf_tensor("bo_hi", [1, D], BF16))
        sgcol = en(nc.sbuf_tensor("sgcol", [128, 64], F32))
        sucol = en(nc.sbuf_tensor("sucol", [128, 64], F32))
        msk = en(nc.sbuf_tensor("msk_sb", [128, 1], F32))
        totals = en(nc.sbuf_tensor("totals", [128, 16], F32))
        initc = en(nc.sbuf_tensor("initc", [128, 16], F32))
        stats = en(nc.sbuf_tensor("stats", [128, 64], F32))
        stage1 = en(nc.sbuf_tensor("stage1", [64, 128], F32))
        dly_v = en(nc.sbuf_tensor("dly_v", [128, 560], F32))
        wnat = en(nc.sbuf_tensor("wnat", [128, 4096], BF16))
        dly_s = en(nc.sbuf_tensor("dly_s", [128, 512], F32))
        psA = en(nc.psum_tensor("psA", [128, 1024], F32))
        psB = en(nc.psum_tensor("psB", [128, 1024], F32))
        psS = en(nc.psum_tensor("psS", [128, 1024], F32))
        psT = en(nc.psum_tensor("psT", [128, 2048], BF16))
        psX = psS
        sem_names = ["t", "v", "s", "g", "cc",
                     "dx", "dck", "dcv", "dcq", "dco", "dcb", "dmisc", "dstg",
                     "dw0", "dw1", "dcc", "dst", "dg0", "dg1", "dg2",
                     "dd0", "dd1", "dsd", "dr0", "dr1", "df0", "df1"]
        sems = {nm: en(nc.semaphore("sem_" + nm)) for nm in sem_names}
        block = en(nc.Block())

        a5b = a5.bitcast(BF16)
        fuse = resid.bitcast(BF16)
        otile = pkv.bitcast(F32)
        scrb = scr.bitcast(BF16)
        wtbf = wtb.bitcast(F32)
        assert list(a5b.shape) == [128, 8192], a5b.shape

        tg1 = E("g", lambda e: e.memset(identb[:], 0.0), inc="g")
        tg2 = E("g", lambda e: e.memset(identf[:], 0.0), inc="g")
        tg3 = E("g", lambda e: e.affine_select(out=identb[:], in_=identb[:],
                                               compare_op=ALU.not_equal, fill=1.0, base=0,
                                               pattern=[[-1, 128]], channel_multiplier=1),
                waits=[tg1], inc="g")
        tg4 = E("g", lambda e: e.affine_select(out=identf[:], in_=identf[:],
                                               compare_op=ALU.not_equal, fill=1.0, base=0,
                                               pattern=[[-1, 128]], channel_multiplier=1),
                waits=[tg2], inc="g")
        t_ident = E("g", lambda e: e.memset(onesb[:], 1.0), waits=[tg3, tg4], inc="g")
        E("v", lambda e: e.memset(dly_v[:], 0.0))
        E("s", lambda e: e.memzero(dly_s[:]))
        E("v", lambda e: e.memset(stats[:, 58:59], 1.0 / D))
        E("v", lambda e: e.memset(stats[:, 59:60], -1.0))
        E("v", lambda e: e.memset(stats[:, 60:61], EPS))
        E("v", lambda e: e.memset(stats[:, 61:62], -0.5))
        t_consts = E("v", lambda e: e.tensor_add(dly_v[:], dly_v[:], dly_v[:]), inc="v")

        for rt in range(8):
            DMA("g", "dx", resid[:, rt * D:(rt + 1) * D],
                xin[rt * 128:(rt + 1) * 128, :])
        t_x = TOK("dx")
        wsrc_of = {"k": wk, "v": wv, "q": wq, "o": wo}
        t_bo_cast = DMA("g", "dcb", bo_bf_d[:, :], bo[:, :])

        DMA("y", "dmisc", msk[:], msk_in[:, :])
        t_msk = TOK("dmisc")

        prep_done = []
        stage_free = None
        for src, n, dst in ((bq, 16, bcol[:, 0:16]), (bk, 16, bcol[:, 16:32]),
                            (bv, 16, bcol[:, 32:48]), (sg, 64, sgcol[:, 0:64]),
                            (su, 64, sucol[:, 0:64])):
            tl = DMA("y", "dstg", stage1[0:n, :], src[0:n, :], waits=[stage_free])
            tt = E("t", lambda e, n_=n: e.transpose(psX[0:128, 0:n_], stage1[0:n_, :],
                                                    identf[0:n_, 0:n_]),
                   waits=[tl, t_ident, TOK("s")], inc="t")
            E("s", lambda e, n_=n, d_=dst: e.copy(d_, psX[:, 0:n_]), waits=[tt])
            ts_ = E("s", lambda e: e.copy(dly_s[:], dly_s[:]), inc="s")
            prep_done.append(ts_)
            stage_free = ts_
        prep_q, prep_k, prep_v, prep_sg, prep_su = prep_done

        def stats_chain(c, width, wv_tok):
            cw = slice(0, 0)
            def a_(out_c, in_c, func, bias=None, scale=None, w=width):
                def fn(e, oc=out_c, ic=in_c, f=func, b=bias, sc=scale, w_=w):
                    kw = {}
                    if b is not None:
                        kw["bias"] = stats[:, b:b + 1] if isinstance(b, int) else b
                    if sc is not None:
                        kw["scale"] = stats[:, sc:sc + 1] if isinstance(sc, int) else sc
                    return e.activation(stats[:, oc:oc + w_], stats[:, ic:ic + w_], f, **kw)
                return fn
            E("s", a_(24 + c, c, AF.Identity, scale=stats[:, 58:59]),
              waits=[wv_tok, t_consts])
            E("s", a_(16 + c, 8 + c, AF.Identity, scale=stats[:, 58:59]))
            E("s", a_(8 + c, 24 + c, AF.Square))
            E("s", a_(8 + c, 8 + c, AF.Identity, scale=stats[:, 59:60]))
            for j in range(width):
                E("s", a_(32 + c + j, 16 + c + j, AF.Identity, bias=8 + c + j, w=1))
            E("s", a_(32 + c, 32 + c, AF.Ln, bias=stats[:, 60:61]))
            E("s", a_(32 + c, 32 + c, AF.Exp, scale=stats[:, 61:62]))
            E("s", a_(48 + c, 24 + c, AF.Identity, scale=stats[:, 59:60]))
            for j in range(width):
                E("s", a_(40 + c + j, 48 + c + j, AF.Identity, scale=32 + c + j, w=1))
            return E("s", lambda e: e.copy(dly_s[:], dly_s[:]), inc="s")

        scr_free = [("v", 0)]

        def ln_sums(c, src_ap, first_waits):
            E("v", lambda e: e.reduce_sum(stats[:, c:c + 1], src_ap, axis=AXX),
              waits=list(first_waits))
            E("s", lambda e: e.activation(scr[:, 0:1024], src_ap[:, 0:1024], AF.Square),
              waits=list(first_waits) + [scr_free[0]])
            sqb = E("s", lambda e: e.activation(scr[:, 1024:2048], src_ap[:, 1024:2048],
                                                AF.Square), inc="s")
            E("v", lambda e: e.reduce_sum(stats[:, 8 + c:9 + c], scr[:], axis=AXX),
              waits=[sqb])
            tok = E("v", lambda e: e.tensor_add(dly_v[:], dly_v[:], dly_v[:]), inc="v")
            scr_free[0] = tok
            return tok

        def ln_stats(c, src_ap, first_waits):
            return stats_chain(c, 1, ln_sums(c, src_ap, first_waits))

        def transpose_16(src_bf, rt, first_wait, psT_free):
            fw = list(first_wait) if isinstance(first_wait, list) else [first_wait]
            last_ev = None
            for b4 in range(4):
                sl = b4 % 2
                tt = None
                for j in range(4):
                    m = b4 * 4 + j
                    tt = E("t", lambda e, i_=src_bf[:, m * 128:(m + 1) * 128],
                           o_=psT[:, sl * 1024 + j * 128:sl * 1024 + (j + 1) * 128]:
                           e.transpose(o_, i_, identb[:]),
                           waits=fw + [psT_free[sl], t_ident],
                           inc="t" if j == 3 else None)
                ev = E("v", lambda e, m0=b4 * 4, r0=rt * 128, sl_=sl: e.tensor_copy(
                    bass.AP(xnt, m0 * R + r0, [[16 * R, 128], [R, 4], [1, 128]]),
                    bass.AP(psT, sl_ * 1024, [[2048, 128], [128, 4], [1, 128]])),
                    waits=[tt], inc="v")
                psT_free[sl] = ev
                last_ev = ev
            return last_ev

        for rt in range(4):
            ln_sums(rt, resid[:, rt * D:(rt + 1) * D], [t_x])
        t_stats1a = stats_chain(0, 4, TOK("v"))
        for rt in range(4, 8):
            ln_sums(rt, resid[:, rt * D:(rt + 1) * D], [t_x])
        t_stats1b = stats_chain(4, 4, TOK("v"))

        psT_free = [TOK("v"), TOK("v")]
        xn_free = [TOK("t"), TOK("t")]
        xnt_rh = [None, None]
        for rt in range(8):
            par = rt % 2
            xnb = a5b[:, par * 2048:(par + 1) * 2048]
            tn = E("s", lambda e, rt_=rt, o_=xnb: e.activation(
                o_, resid[:, rt_ * D:(rt_ + 1) * D], AF.Identity,
                bias=stats[:, 40 + rt_:41 + rt_], scale=stats[:, 32 + rt_:33 + rt_]),
                waits=[t_stats1a if rt < 4 else t_stats1b, xn_free[par]], inc="s")
            ev = transpose_16(xnb, rt, tn, psT_free)
            xn_free[par] = TOK("t")
            if rt == 3:
                xnt_rh[0] = ev
            if rt == 7:
                xnt_rh[1] = ev
        bar_p1 = [TOK("t"), TOK("v"), TOK("s")]
        if dbg:
            bar_p1.append(DMA("y", "dmisc", dbg_xnt[:, :], xnt[:], waits=bar_p1))

        _lvl = {"p1": 1, "p2": 2, "p3": 3, None: 9}[stop]
        if dbg and _lvl == 1:
            for rt_ in range(8):
                DMA("y", "dmisc", dbg_x2[rt_ * 128:(rt_ + 1) * 128, :],
                    resid[:, rt_ * D:(rt_ + 1) * D], waits=bar_p1)
            DMA("y", "dmisc", dbg_tot[:, :], stats[:, 0:16], waits=bar_p1)
            DMA("y", "dmisc", dbg_ini[:, :], stats[:, 24:40], waits=bar_p1)
            DMA("y", "dmisc", dbg_ctx[:, 0:32], stats[:, 48:64].bitcast(BF16), waits=bar_p1)
            DMA("y", "dmisc", dbg_pkv[:, 0:4096], a5b[:, 0:4096], waits=bar_p1)
        if _lvl >= 2:
            wslot_free = [("t", 0), ("t", 0)]
            wnat_free = [("t", 0), ("t", 0)]

            def load_wt_group_pe(key, g, p):
                wsrc = wsrc_of[key]
                ev = None
                for ib in range(4):
                    eb = g * 4 + ib
                    ns = eb % 2
                    ld = DMA("g", "dg%d" % ns,
                             wnat[:, ns * 2048:(ns + 1) * 2048],
                             wsrc[eb * 128:(eb + 1) * 128, :],
                             waits=[wnat_free[ns]])
                    for tb in range(4):
                        psl = tb % 2
                        tt = None
                        for j in range(4):
                            t_ = tb * 4 + j
                            tt = E("t", lambda e, ns_=ns, t2=t_, psl_=psl, j_=j:
                                   e.transpose(
                                       psT[:, psl_ * 1024 + j_ * 128:
                                           psl_ * 1024 + (j_ + 1) * 128],
                                       wnat[:, ns_ * 2048 + t2 * 128:
                                            ns_ * 2048 + (t2 + 1) * 128],
                                       identb[:]),
                                   waits=[ld, psT_free[psl], t_ident],
                                   inc="t" if j == 3 else None)
                        ev = E("v", lambda e, p_=p, tb_=tb, ib_=ib, psl_=psl:
                               e.tensor_copy(
                                   bass.AP(wtb, p_ * 8192 + tb_ * 4 * 512 + ib_ * 128,
                                           [[16384, 128], [512, 4], [1, 128]]),
                                   bass.AP(psT, psl_ * 1024,
                                           [[2048, 128], [128, 4], [1, 128]])),
                               waits=[tt, wslot_free[p] if (ib == 0 and tb == 0) else None],
                               inc="v")
                        psT_free[psl] = ev
                    wnat_free[ns] = tt
                return ev

            def load_wt_group(wbf_dram, g, extra_wait):
                p = g % 2
                first = True
                tok = None
                for d in range(16):
                    tok = DMA("y", "dw%d" % p,
                              wtb[:, p * 8192 + d * 512:p * 8192 + d * 512 + 512],
                              wbf_dram[g * 512:(g + 1) * 512, d * 128:(d + 1) * 128],
                              waits=[wslot_free[p], extra_wait] if first else (),
                              transpose=True)
                    first = False
                return tok

            def proj_mms(m, ps, extra_waits):
                p = (m // 4) % 2
                last = None
                for rh in range(2):
                    for d in range(16):
                        if d == 0:
                            ws = ([xnt_rh[rh]] + (list(extra_waits) if rh == 0 else []))
                        else:
                            ws = ()
                        last = E("t", lambda e, d_=d, rh_=rh, m_=m, p_=p, ps_=ps: e.matmul(
                            ps_[:, rh_ * 512:(rh_ + 1) * 512],
                            wtb[:, p_ * 8192 + d_ * 512 + (m_ % 4) * 128:
                                p_ * 8192 + d_ * 512 + (m_ % 4) * 128 + 128],
                            xnt[:, d_ * R + rh_ * 512:d_ * R + rh_ * 512 + 512],
                            start=(d_ == 0), stop=(d_ == 15)),
                            waits=ws,
                            inc="t" if (rh == 1 and d == 15) else None)
                return last

            ps_free = {0: ("v", 0), 1: ("v", 0), 2: ("v", 0)}
            PSROT = [psA, psB, psS]
            TEXP = [a5b[:, 0:1024], a5b[:, 1024:2048]]
            TRELU = a5b[:, 2048:3072]
            TV = [a5b[:, 3072:4096], a5b[:, 4096:5120]]
            PHIQ = a5b[:, 5120:6144]
            texp_free = [TOK("v"), TOK("v")]
            tv_free = [TOK("v"), TOK("v")]

            def wt_pass(key, evac, first_extra, preloaded=None, nrot=3):
                loaded = preloaded if preloaded is not None else {
                    0: load_wt_group_pe(key, 0, 0),
                    1: load_wt_group_pe(key, 1, 1)}
                for m in range(16):
                    g = m // 4
                    sl = m % nrot
                    ps = PSROT[sl]
                    extra = [loaded[g], ps_free[sl]] + (first_extra if m == 0 else [])
                    tmm = proj_mms(m, ps, extra)
                    if m % 4 == 3:
                        wslot_free[g % 2] = tmm
                        if g + 2 < 4:
                            loaded[g + 2] = load_wt_group_pe(key, g + 2, g % 2)
                    evac(m, ps, tmm, sl)

            def evac_k(m, ps, tmm, sl):
                par = m % 2
                te = E("s", lambda e, ps_=ps, m_=m, p_=par: e.activation(
                    TEXP[p_], ps_[:], AF.Exp, bias=bcol[:, 16 + m_:17 + m_]),
                    waits=[tmm, texp_free[par], prep_k], inc="s")
                tr = E("v", lambda e, ps_=ps, m_=m: e.tensor_scalar(
                    TRELU, ps_[:], bcol[:, 16 + m_:17 + m_], 0.0, ALU.add, ALU.max),
                    waits=[te], inc="v")
                ps_free[sl] = tr
                ph = E("v", lambda e, m_=m, p_=par: e.scalar_tensor_tensor(
                    pkv[:, m_ * R:(m_ + 1) * R], TEXP[p_], 1.0, TRELU, ALU.min, ALU.add),
                    inc="v")
                texp_free[par] = ph

            wt_pass("k", evac_k, [])

            tot_toks = []

            def evac_v(m, ps, tmm, sl):
                par = m % 2
                tv_ = E("s", lambda e, ps_=ps, m_=m, p_=par: e.activation(
                    TV[p_], ps_[:], AF.Identity, bias=bcol[:, 32 + m_:33 + m_]),
                    waits=[tmm, tv_free[par], prep_v], inc="s")
                ps_free[sl] = tv_
                mu = E("v", lambda e, m_=m, p_=par: e.tensor_mul(
                    pkv[:, m_ * R:(m_ + 1) * R], pkv[:, m_ * R:(m_ + 1) * R], TV[p_]),
                    waits=[tv_], inc="v")
                tv_free[par] = mu
                E("v", lambda e, m_=m: e.reduce_sum(
                    totals[:, m_:m_ + 1], pkv[:, m_ * R:(m_ + 1) * R], axis=AXX))
                if m == 15:
                    tot_toks.append(E("v", lambda e: e.tensor_add(dly_v[:], dly_v[:], dly_v[:]),
                                     inc="v"))
                else:
                    tot_toks.append(TOK("v"))

            wt_pass("v", evac_v, [])

            psS_clear = ps_free[2]
            q_preload = {0: load_wt_group_pe("q", 0, 0),
                         1: load_wt_group_pe("q", 1, 1)}

            if dbg:
                dtok = DMA("y", "dmisc", dbg_pkv[:, :], pkv[:], waits=[tot_toks[-1]])
                DMA("y", "dmisc", dbg_tot[:, :], totals[:], waits=[tot_toks[-1]])

            d1 = DMA("y", "dcc", cc_in[:, :], totals[:, 0:16], waits=[tot_toks[-1]])
            ccd = E("g", lambda e: e.collective_compute(
                "AllGather", ALU.bypass, replica_groups=[[0, 1], [2, 3], [4, 5], [6, 7]],
                ins=[cc_in.ap().opt()], outs=[cc_out.ap().opt()]),
                waits=[d1], inc="cc")
            d2 = DMA("y", "dcc", initc[:, 0:16], cc_out[0:128, :], waits=[ccd])
            E("s", lambda e: e.activation(initc[:, 0:16], initc[:, 0:16], AF.Identity,
                                          scale=msk[:]),
              waits=[d2, t_msk])
            t_init = E("s", lambda e: e.copy(dly_s[:], dly_s[:]), inc="s")
            t_bo = DMA("y", "dmisc", bo_hi[:], bo_bf_d[:, :], waits=[t_bo_cast])
            if dbg:
                DMA("y", "dmisc", dbg_ini[:, :], initc[:], waits=[t_init])

            ctx_last = [None]

            def evac_q(m, ps, tmm, sl):
                par = m % 2
                te = E("s", lambda e, ps_=ps, m_=m, p_=par: e.activation(
                    TEXP[p_], ps_[:], AF.Exp, bias=bcol[:, m_:1 + m_]),
                    waits=[tmm, texp_free[par], prep_q], inc="s")
                tr = E("v", lambda e, ps_=ps, m_=m: e.tensor_scalar(
                    TRELU, ps_[:], bcol[:, m_:1 + m_], 0.0, ALU.add, ALU.max),
                    waits=[te], inc="v")
                ps_free[sl] = tr
                E("v", lambda e, p_=par: e.scalar_tensor_tensor(
                    PHIQ, TEXP[p_], 1.0, TRELU, ALU.min, ALU.add))
                E("v", lambda e, m_=m: e.tensor_tensor_scan(
                    psS[:], pkv[:, m_ * R:(m_ + 1) * R], pkv[:, m_ * R:(m_ + 1) * R],
                    initc[:, m_:m_ + 1], ALU.add, ALU.bypass),
                    waits=[t_init, psS_clear] + ([dtok] if dbg else []))
                cx = E("v", lambda e, m_=m: e.scalar_tensor_tensor(
                    pkv[:, m_ * R:(m_ + 1) * R], psS[:], SCALE, PHIQ, ALU.mult, ALU.mult),
                    inc="v")
                texp_free[par] = cx
                ctx_last[0] = cx

            wt_pass("q", evac_q, [psS_clear], preloaded=q_preload, nrot=2)
            bar_ctx = ctx_last[0]


        if _lvl >= 3:
            PS3 = [psA[:, 0:512], psA[:, 512:1024], psB[:, 0:512], psB[:, 512:1024]]
            ps3_free = {i: TOK("v") for i in range(4)}
            x2_store = []
            tnorm_toks = []
            attn_bf = xnt

            wo_loaded = {0: load_wt_group_pe("o", 0, 0),
                         1: load_wt_group_pe("o", 1, 1)}
            evac_toks = {rt: [] for rt in range(8)}
            for g in range(4):
                p = g % 2
                last = None
                for rt in range(8):
                    slot = rt % 4
                    for d in range(16):
                        E("t", lambda e, d_=d, rt_=rt, p_=p, sl=slot: e.matmul(
                            PS3[sl],
                            pkv[:, d_ * R + rt_ * 128:d_ * R + rt_ * 128 + 128],
                            wtb[:, p_ * 8192 + d_ * 512:p_ * 8192 + d_ * 512 + 512],
                            start=(d_ == 0), stop=False),
                            waits=[wo_loaded[g], ps3_free[slot], bar_ctx, t_bo]
                            if d == 0 else ())
                    last = E("t", lambda e, g_=g, sl=slot: e.matmul(
                        PS3[sl], onesb[0:1, :], bo_hi[0:1, g_ * 512:(g_ + 1) * 512],
                        start=False, stop=True), inc="t")
                    ev = E("s", lambda e, g_=g, rt_=rt, sl=slot: e.copy(
                        attn_bf[:, rt_ * 2048 + g_ * 512:rt_ * 2048 + g_ * 512 + 512],
                        PS3[sl]), waits=[last], inc="s")
                    ps3_free[slot] = ev
                    evac_toks[rt].append(ev)
                    if g == 3:
                        if dbg and rt == 1:
                            datt = DMA("y", "dmisc", dbg_att[:, :],
                                       attn_bf[:, 0:4096],
                                       waits=evac_toks[0] + evac_toks[1])
                        c = rt % 2
                        at = attn_bf[:, rt * 2048:(rt + 1) * 2048]
                        tn2 = ln_stats(c, at, evac_toks[rt] +
                                       ([datt] if (dbg and rt >= 2) else []))
                        E("v", lambda e, a_=at, c_=c: e.tensor_scalar(
                            scr[:], a_, stats[:, 48 + c_:49 + c_],
                            stats[:, 32 + c_:33 + c_],
                            ALU.add, ALU.mult), waits=[tn2])
                        add_t = E("v", lambda e, rt_=rt: e.tensor_add(
                            resid[:, rt_ * D:(rt_ + 1) * D],
                            resid[:, rt_ * D:(rt_ + 1) * D], scr[:]), inc="v")
                        scr_free[0] = add_t
                        x2_store.append(DMA("y", "dst", y[rt * 128:(rt + 1) * 128, :],
                                            resid[:, rt * D:(rt + 1) * D],
                                            waits=[add_t]))
                        tn3 = ln_stats(2 + c, resid[:, rt * D:(rt + 1) * D], [add_t])
                        xn2b = fuse[:, rt * 4096:rt * 4096 + 2048]
                        tnorm = E("s", lambda e, rt_=rt, o_=xn2b, c_=2 + c: e.activation(
                            o_, resid[:, rt_ * D:(rt_ + 1) * D], AF.Identity,
                            bias=stats[:, 40 + c_:41 + c_],
                            scale=stats[:, 32 + c_:33 + c_]),
                            waits=[tn3, x2_store[-1]], inc="s")
                        tnorm_toks.append(tnorm)
                wslot_free[p] = last
                if g + 2 < 4:
                    wo_loaded[g + 2] = load_wt_group_pe("o", g + 2, g % 2)

            t_applies = TOK("v")
            xn2T_done = []
            xnt2_rh = [None, None]
            for rt in range(8):
                xn2T_done.append(transpose_16(fuse[:, rt * 4096:rt * 4096 + 2048], rt,
                                              [tnorm_toks[rt], t_applies], psT_free))
            xnt2_rh[0] = xn2T_done[3]
            xnt2_rh[1] = xn2T_done[7]

            bar_p3 = [TOK("t"), TOK("v"), TOK("s"), x2_store[-1]]
            if dbg:
                bar_p3.append(DMA("y", "dmisc", dbg_ctx[:, :], pkv[:], waits=[bar_ctx]))
                bar_p3.append(DMA("y", "dmisc", dbg_xn2[:, :], xnt[:], waits=bar_p3[:4]))
                for rt_ in range(8):
                    tok_ = DMA("y", "dmisc", dbg_x2[rt_ * 128:(rt_ + 1) * 128, :],
                               resid[:, rt_ * D:(rt_ + 1) * D], waits=bar_p3[:4])
                bar_p3.append(tok_)


        if _lvl >= 9:
            GU = [(wtb, 0), (wtb, 8192), (a5b, 0)]
            TG = [scrb[:, 0:512], scrb[:, 512:1024]]
            TU = [scrb[:, 1024:1536], scrb[:, 1536:2048]]

            def load_gu(wdram, s_, slot, wait_tok):
                parent, base = GU[slot]
                ncols = parent.shape[1]
                first = True
                tok = None
                for t2 in range(2):
                    tok = DMA("g", "dg%d" % slot,
                              bass.AP(parent, base + t2 * 8 * 512,
                                      [[ncols, 128], [512, 8], [1, 512]]),
                              bass.AP(wdram, t2 * 8 * 128 * INTER + s_ * 512,
                                      [[INTER, 128], [128 * INTER, 8], [1, 512]]),
                              waits=[wait_tok] if first else ())
                    first = False
                return tok

            def gu_slice(slot, d, icw):
                parent, base = GU[slot]
                return parent[:, base + d * 512 + icw * 128: base + d * 512 + icw * 128 + 128]

            bar_h = bar_p3
            h_slotC = bar_p3[0]
    
            x2l_free = [None, None]
            wd_last_prev = {0: bar_p3[0], 1: bar_p3[0]}
            psd_prev = None
            for h in range(2):
                if h == 0:
                    slot_last = {0: wslot_free[0], 1: wslot_free[1], 2: bar_ctx}
                else:
                    slot_last = {0: wd_last_prev[0], 1: wd_last_prev[1], 2: h_slotC}
                loaded = {(0, 0): load_gu(wg, 0, 0, slot_last[0]),
                          (0, 1): load_gu(wu, 0, 1, slot_last[1])}
                slot_of = {(0, 0): 0, (0, 1): 1}
                for s_ in range(1, 16):
                    slot_of[(s_, 0)] = (2 * s_) % 3
                    slot_of[(s_, 1)] = (2 * s_ + 1) % 3
                tg_free = [TOK("v"), TOK("v")]
                tu_free = [TOK("v"), TOK("v")]
                if psd_prev is None:
                    psg_free = [TOK("s"), TOK("s")]
                    psu_free = [TOK("v"), TOK("v")]
                else:
                    psg_free = [psd_prev[0], psd_prev[2]]
                    psu_free = [psd_prev[1], psd_prev[3]]
                fuse_last = [None]
                for s_ in range(16):
                    slg, slu = slot_of[(s_, 0)], slot_of[(s_, 1)]
                    lastg = lastu = None
                    for icw in range(4):
                        ic = s_ * 4 + icw
                        par = ic % 2
                        psg = (psA if par == 0 else psB)[:, 0:512]
                        psu = (psA if par == 0 else psB)[:, 512:1024]
                        for d in range(16):
                            lastg = E("t", lambda e, d_=d, sl=slg, ic_=icw, ps_=psg, h_=h: e.matmul(
                                ps_, gu_slice(sl, d_, ic_),
                                xnt[:, d_ * R + h_ * 512:d_ * R + h_ * 512 + 512],
                                start=(d_ == 0), stop=(d_ == 15)),
                                waits=([loaded[(s_, 0)], psg_free[par]] +
                                       ([xnt2_rh[h]] if (s_ == 0 and icw == 0) else []))
                                if d == 0 else (),
                                inc="t" if d == 15 else None)
                        for d in range(16):
                            lastu = E("t", lambda e, d_=d, sl=slu, ic_=icw, ps_=psu, h_=h: e.matmul(
                                ps_, gu_slice(sl, d_, ic_),
                                xnt[:, d_ * R + h_ * 512:d_ * R + h_ * 512 + 512],
                                start=(d_ == 0), stop=(d_ == 15)),
                                waits=[loaded[(s_, 1)], psu_free[par]] if d == 0 else (),
                                inc="t" if d == 15 else None)
                        tg_t = E("s", lambda e, ps_=psg, ic_=ic, p_=par: e.activation(
                            TG[p_], ps_, AF.Silu, scale=sgcol[:, ic_:ic_ + 1]),
                            waits=[lastg, tg_free[par], prep_sg], inc="s")
                        psg_free[par] = tg_t
                        tu_t = E("v", lambda e, ps_=psu, ic_=ic, p_=par: e.tensor_scalar(
                            TU[p_], ps_, sucol[:, ic_:ic_ + 1], None, ALU.mult),
                            waits=[lastu, tu_free[par], prep_su], inc="v")
                        psu_free[par] = tu_t
                        f_t = E("v", lambda e, ic_=ic, p_=par: e.tensor_mul(
                            fuse[:, ic_ * 512:(ic_ + 1) * 512], TG[p_], TU[p_]),
                            waits=[tg_t], inc="v")
                        tg_free[par] = f_t
                        tu_free[par] = f_t
                        fuse_last[0] = f_t
                    slot_last[slg] = lastg
                    slot_last[slu] = lastu
                    if s_ + 1 < 16:
                        ng, nu = slot_of[(s_ + 1, 0)], slot_of[(s_ + 1, 1)]
                        loaded[(s_ + 1, 0)] = load_gu(wg, s_ + 1, ng, slot_last[ng])
                        loaded[(s_ + 1, 1)] = load_gu(wu, s_ + 1, nu, slot_last[nu])
                bar_4a = [TOK("t"), TOK("v"), TOK("s")]

                sdb = wtbf[:, 2048:4096]
                t_sdb = DMA("g", "dsd", sdb, bass.AP(sd, 0, [[0, 128], [1, 2048]]),
                            waits=[bar_4a[0]])
                WDSL = [(0, None), (2048, None)]

                def load_wd(s_, dgi, wait_tok):
                    p = s_ % 2
                    return DMA("g", "dd%d" % p,
                               bass.AP(wtb, WDSL[p][0], [[16384, 128], [512, 4], [1, 512]]),
                               bass.AP(wd, s_ * 512 * D + dgi * 512,
                                       [[D, 128], [128 * D, 4], [1, 512]]),
                               waits=[wait_tok])

                PSD = [psA[:, 0:512], psA[:, 512:1024], psB[:, 0:512], psB[:, 512:1024]]
                psd_free = {i: bar_4a[1] for i in range(4)}
                wd_last = {0: bar_4a[0], 1: bar_4a[0]}
                ffn_evac = {rs: [] for rs in range(4)}
                for dgi in range(4):
                    wloaded = {0: load_wd(0, dgi, wd_last[0]),
                               1: load_wd(1, dgi, wd_last[1])}
                    group_end = {}
                    for s_ in range(16):
                        p = s_ % 2
                        for icw in range(4):
                            ic = s_ * 4 + icw
                            for rs in range(4):
                                first = (s_ == 0 and icw == 0)
                                lastmm = (s_ == 15 and icw == 3)
                                if first:
                                    ws = [wloaded[p], psd_free[rs]] + \
                                         (bar_4a + [fuse_last[0]] if (rs == 0 and dgi == 0) else [])
                                elif icw == 0 and rs == 0:
                                    ws = [wloaded[p]]
                                else:
                                    ws = ()
                                tok = E("t", lambda e, ic_=ic, rs_=rs, p_=p, ic2=icw,
                                        fl=first, ll=lastmm: e.matmul(
                                    PSD[rs_],
                                    fuse[:, ic_ * 512 + rs_ * 128:ic_ * 512 + rs_ * 128 + 128],
                                    wtb[:, WDSL[p_][0] + ic2 * 512:WDSL[p_][0] + ic2 * 512 + 512],
                                    start=fl, stop=ll),
                                    waits=ws,
                                    inc="t" if (icw == 3 and rs == 3) else None)
                                if icw == 3 and rs == 3:
                                    group_end[s_] = tok
                        wd_last[p] = group_end[s_]
                        if s_ + 2 < 16:
                            wloaded[p] = load_wd(s_ + 2, dgi, wd_last[p])
                    for rs in range(4):
                        ev = E("v", lambda e, rs_=rs, dg_=dgi: e.scalar_tensor_tensor(
                            otile[:, rs_ * 2048 + dg_ * 512:rs_ * 2048 + dg_ * 512 + 512],
                            PSD[rs_], 1.0, sdb[:, dg_ * 512:(dg_ + 1) * 512],
                            ALU.bypass, ALU.mult),
                            waits=[group_end[15], t_sdb], inc="v")
                        ffn_evac[rs].append(ev)
                        psd_free[rs] = ev

                for rs in range(4):
                    rt = h * 4 + rs
                    par = rs % 2
                    ffn_t = otile[:, rs * 2048:(rs + 1) * 2048]
                    x2l = a5[:, par * 2048:(par + 1) * 2048]
                    rl = DMA("y", "dr%d" % par, x2l, y[rt * 128:(rt + 1) * 128, :],
                             waits=[TOK("dst"), x2l_free[par], bar_4a[0]])
                    tn4 = ln_stats(rs % 2, ffn_t, ffn_evac[rs])
                    E("v", lambda e, f_=ffn_t, c=rs % 2: e.tensor_scalar(
                        f_, f_, stats[:, 48 + c:49 + c], stats[:, 32 + c:33 + c],
                        ALU.add, ALU.mult), waits=[tn4])
                    fin = E("v", lambda e, f_=ffn_t, x_=x2l: e.tensor_add(x_, x_, f_),
                            waits=[rl], inc="v")
                    st = DMA("y", "df%d" % par, y[rt * 128:(rt + 1) * 128, :], x2l,
                             waits=[fin])
                    x2l_free[par] = st
                    if rs >= 2:
                        h_slotC = st
                wd_last_prev = {0: psd_free[3], 1: psd_free[3]}
                psd_prev = [psd_free[i] for i in range(4)]


        g_sems = ["dx", "dcb", "dg0", "dg1", "dg2", "dd0", "dd1", "dsd"]
        y_sems = ["dmisc", "dstg", "dw0", "dw1", "dcc", "dst", "dr0", "dr1",
                  "df0", "df1"]
        E("g", lambda e: e.nop(), waits=[TOK(sm) for sm in g_sems])
        E("y", lambda e: e.nop(), waits=[TOK(sm) for sm in y_sems])

        def runner(name, eng):
            seen = {}
            for fn, ws, inc, amt in plan.steps[name]:
                for s2, v2 in ws:
                    if seen.get(s2, 0) < v2:
                        eng.wait_ge(sems[s2], v2)
                        seen[s2] = v2
                ins = fn(eng)
                if inc is not None:
                    ins.then_inc(sems[inc], amt)

        @block.gpsimd
        def _(g_):
            runner("g", g_)

        @block.sync
        def _(y_):
            runner("y", y_)

        @block.tensor
        def _(t_):
            runner("t", t_)

        @block.vector
        def _(v_):
            runner("v", v_)

        @block.scalar
        def _(s_):
            runner("s", s_)

    return nc


_NC = None


def kernel(**inputs):
    global _NC
    if _NC is None:
        _NC = build_program()
    nc = _NC

    x = np.asarray(inputs["L0_attn_x"])
    base = {
        "wq": np.asarray(inputs["L0_attn_Wq"]),
        "wk": np.asarray(inputs["L0_attn_Wk"]),
        "wv": np.asarray(inputs["L0_attn_Wv"]),
        "wo": np.asarray(inputs["L0_attn_Wo"]),
        "bq": np.ascontiguousarray(np.asarray(inputs["L0_attn_bq"]).reshape(16, 128)),
        "bk": np.ascontiguousarray(np.asarray(inputs["L0_attn_bk"]).reshape(16, 128)),
        "bv": np.ascontiguousarray(np.asarray(inputs["L0_attn_bv"]).reshape(16, 128)),
        "bo": np.ascontiguousarray(np.asarray(inputs["L0_attn_bo"]).reshape(1, D)),
        "wg": np.asarray(inputs["L0_ffn_w_gate"]),
        "wu": np.asarray(inputs["L0_ffn_w_up"]),
        "wd": np.asarray(inputs["L0_ffn_w_down"]),
        "sg": np.ascontiguousarray(np.asarray(inputs["L0_ffn_s_gate"]).reshape(64, 128)),
        "su": np.ascontiguousarray(np.asarray(inputs["L0_ffn_s_up"]).reshape(64, 128)),
        "sd": np.ascontiguousarray(np.asarray(inputs["L0_ffn_s_down"]).reshape(1, D)),
    }
    in_maps = []
    for c in range(8):
        b, h = c // 2, c % 2
        m = dict(base)
        m["x"] = np.ascontiguousarray(x[b, h * R:(h + 1) * R, :])
        m["msk"] = np.full((128, 1), float(h), dtype=np.float32)
        in_maps.append(m)

    res = run_bass_kernel_spmd(nc, in_maps, core_ids=list(range(8)))
    out = np.empty((4, 2048, D), dtype=np.float32)
    for c in range(8):
        b, h = c // 2, c % 2
        out[b, h * R:(h + 1) * R, :] = res.results[c]["y"]
    return out
